# version 09
# baseline (speedup 1.0000x reference)
import contextlib
import numpy as np

import concourse.bass as bass
import concourse.mybir as mybir
from concourse.bass_utils import run_bass_kernel_spmd

F32 = mybir.dt.float32
F32R = mybir.dt.float32r
BF16 = mybir.dt.bfloat16
I32 = mybir.dt.int32
AF = mybir.ActivationFunctionType
ALU = mybir.AluOpType

H = 2048
L = 4096
TL = 1024
HALO = 4
TC = TL + HALO
NKC = 16
DTR = 64
NST = 16
DFF = 1408
NFC = 11
EPS = 1e-5

ENGS = ("sync", "gpsimd", "scalar", "vector", "tensor")
GROUPS = [[0, 1, 2, 3], [4, 5, 6, 7]]
FASZ = 41984


class Sem:
    def __init__(self, h):
        self.h = h
        self.v = 0

    def bump(self, k):
        self.v += k
        return self.v


QDEPTH = 8


class Plan:

    def __init__(self, dnsems, rotsems, esems, ccsem):
        self.ccs = ccsem
        self.q = {e: [] for e in ENGS}
        self._lastwait = {}
        self.dn = dnsems
        self.rots = rotsems
        self.esem = esems
        self.pending = {"sync": [], "gpsimd": []}
        self.echoed = {"sync": 0, "gpsimd": 0}
        self.issued = {"sync": 0, "gpsimd": 0}
        self.gwait = {"sync": [], "gpsimd": []}

    def _emit_echo(self, qn):
        h, v = self.pending[qn].pop(0)
        self.q[qn].append(lambda e, h=h, v=v: e.wait_ge(h, v))
        dn = self.dn[qn]
        self.q[qn].append(lambda e, s=dn.h: e.sem_inc(s, 1))
        dn.bump(1)
        self.echoed[qn] += 1

    def _flush_to(self, qn, val):
        while self.echoed[qn] < val and self.pending[qn]:
            self._emit_echo(qn)

    def _drain_gwaits(self, qn):
        for fn in self.gwait[qn]:
            self.q[qn].append(fn)
        self.gwait[qn] = []

    def _pre_dma(self, qn):
        if len(self.pending[qn]) >= QDEPTH:
            self._emit_echo(qn)
        self._drain_gwaits(qn)

    def dma(self, eng, sem, out, in_, **kw):
        qn = eng
        self._pre_dma(qn)
        rot = self.rots[qn][self.issued[qn] % len(self.rots[qn])]
        self.issued[qn] += 1
        v = rot.bump(16)
        self.q[qn].append(lambda e, out=out, in_=in_, kw=dict(kw), s=rot.h:
                          e.dma_start(out=out, in_=in_, **kw).then_inc(s, 16))
        self.pending[qn].append((rot.h, v))
        return self.issued[qn]

    def idma(self, eng, sem, out, in_, idx, element_offset=0):
        qn = eng
        self._pre_dma(qn)
        rot = self.rots[qn][self.issued[qn] % len(self.rots[qn])]
        self.issued[qn] += 1
        v = rot.bump(16)
        self.q[qn].append(lambda e, out=out, in_=in_, idx=idx, eo=element_offset, s=rot.h:
                          e.indirect_dma_start(out=out, out_offset=None, in_=in_,
                                               in_offset=bass.IndirectOffsetOnAxis(ap=idx, axis=0),
                                               element_offset=eo).then_inc(s, 16))
        self.pending[qn].append((rot.h, v))
        return self.issued[qn]

    def cc(self, eng, sem, kind, in_ap, out_ap):
        qn = eng
        self._pre_dma(qn)
        rot = self.ccs
        self.issued[qn] += 1
        v = rot.bump(1)
        self.q[qn].append(lambda e, k=kind, i=in_ap, o=out_ap, s=rot.h:
                          e.collective_compute(k, ALU.bypass, replica_groups=GROUPS,
                                               ins=[i.opt()], outs=[o.opt()]).then_inc(s, 1))
        self.pending[qn].append((rot.h, v))
        return self.issued[qn]

    def op(self, eng, sem, fn):
        if eng == "gpsimd":
            self._drain_gwaits("gpsimd")
        if eng in ("vector", "scalar", "gpsimd"):
            es = self.esem[eng]
            if es.v > 0:
                self.q[eng].append(lambda e, s=es.h, v=es.v: e.wait_ge(s, v))
            v = es.bump(1)
            self.q[eng].append(lambda e, fn=fn, s=es.h: fn(e).then_inc(s, 1))
            return v
        if sem is None:
            self.q[eng].append(fn)
            return None
        v = sem.bump(1)
        self.q[eng].append(lambda e, fn=fn, s=sem.h: fn(e).then_inc(s, 1))
        return v

    def wait(self, eng, sem, val):
        if val is None or val <= 0:
            return
        for qn in ("sync", "gpsimd"):
            if sem is self.dn[qn]:
                self._flush_to(qn, val)
                break
        key = (eng, id(sem))
        if self._lastwait.get(key, -1) >= val:
            return
        self._lastwait[key] = val
        fn = lambda e, s=sem.h, v=val: e.wait_ge(s, v)
        if eng in ("sync", "gpsimd"):
            self.gwait[eng].append(fn)
        else:
            self.q[eng].append(fn)

    def finish(self):
        for qn in ("sync", "gpsimd"):
            while self.pending[qn]:
                self._emit_echo(qn)
            self._drain_gwaits(qn)


def build_nc():
    nc = bass.Bass("TRN2", target_bir_lowering=False)

    def din(name, shape, dt):
        return nc.dram_tensor(name, shape, dt, kind="ExternalInput").ap()

    def dint(name, shape, dt):
        return nc.dram_tensor(name, shape, dt).ap()

    x_fm = din("x_fm", [H, TC], F32R)
    wi = din("wi", [32 * 128, NKC * 128], F32R)
    wc = din("wc", [16 * 128, 4 * NKC * 128], F32R)
    wo = din("wo", [16 * 128, NKC * 128], F32R)
    xpv = din("xpv", [128, NKC * 96], F32R)
    dtw = din("dtw", [DTR, 512], F32)
    dtb = din("dtb", [128, 4], F32)
    alog = din("alog", [128, 4 * NST], F32)
    dvec = din("dvec", [128, 4], F32)
    nw = din("nw", [128, NKC], F32)
    cbv = din("cbv", [128, NKC], F32)
    gwv = din("gwv", [128, NKC * 6], F32)
    w1f = din("w1f", [6 * H, DFF], F32)
    w2f = din("w2f", [6 * DFF, H], F32)
    b1fm = din("b1fm", [6 * 128, NFC], F32)
    b2fm = din("b2fm", [6 * 128, NKC], F32)
    identi = din("identi", [128, 128], F32)
    onesf = din("onesf", [128, 1], F32)
    iota8 = din("iota8", [1, 8], F32)
    qoff = din("qoff", [1, 2], I32)
    cvals = din("cvals", [1, 2], F32)

    out_fm = nc.dram_tensor("out_fm", [H, TL], F32, kind="ExternalOutput").ap()

    zdram = dint("zdram", [H, TL], F32)
    xcst = dint("xcst", [H, TL], F32)
    dbcst = dint("dbcst", [96, TL], F32)
    xcg = dint("xcg", [4 * H, TL], F32)
    dbcg = dint("dbcg", [4 * 96, TL], F32)
    bmcmd = dint("bmcmd", [32, L], F32R)
    yst = dint("yst", [512, L], F32)
    yg = dint("yg", [H, L], F32)
    ygv = yg.rearrange("c (r t) -> (c r) t", t=TL)
    gst = dint("gst", [128, NKC], F32)
    gg = dint("gg", [512, NKC], F32)
    rowb = dint("rowb", [1, 2 * TC], F32)
    wrow = dint("wrow", [1, 8], F32)

    with contextlib.ExitStack() as ctx:
        sbt = lambda n, s, t: ctx.enter_context(nc.sbuf_tensor(n, s, t))
        ARENA = sbt("ARENA", [128, FASZ * 4], mybir.dt.uint8)
        _abase = nc.lookup_mloc(ARENA).addr

        def alias(name, shape, dtype, byte_off):
            h = nc.virtual_sbuf_tensor(name, shape, dtype)
            nc._tensor(h.name, h.shape, h.dtype, type="SB")
            m = nc.lookup_mloc(h)
            m.allocated = True
            m.addr = _abase + byte_off
            return h

        FRt = alias("FRt", [128, 32896], F32R, 0)
        SQRt = alias("SQRt", [128, 1028], F32R, 32896 * 4)
        RBRt = alias("RBRt", [128, 1028], F32, 33928 * 4)
        WINRt = alias("WINRt", [128, 4976], F32R, 34960 * 4)
        ZTRt = alias("ZTRt", [128, 2048], F32, 39936 * 4)
        ABf = alias("ABf", [128, 2048], F32, 16384 * 4)
        BTf = alias("BTf", [128, 1024], F32, 18432 * 4)
        STf = alias("STf", [128, 2048], F32, 29696 * 4)
        SCf = alias("SCf", [128, 4096], F32, 19456 * 4)
        RB1f = alias("RB1f", [128, 1024], F32, 32768 * 4)
        MRBf = alias("MRBf", [128, 1024], F32, 33792 * 4)
        T1f = alias("T1f", [128, 1024], F32, 34816 * 4)
        XPWr = alias("XPWr", [128, 1536], F32R, 39936 * 4)
        GB = alias("GB", [128, 11264], BF16, 0)
        X2B = alias("X2B", [128, 16384], BF16, 16384 * 2)
        W1B = alias("W1B", [128, 22528], BF16, 34816 * 2)

        BA = sbt("BA", [128, 16384], BF16)
        SM = sbt("SM", [128, 1088], F32)
        SR = sbt("SR", [128, 768], F32R)
        SI = sbt("SI", [128, 64], I32)
        R8 = sbt("R8", [1, 64], F32)
        onesb = sbt("onesb", [128, 1], BF16)
        dtbs = SM[:, 0:4]
        alogs = SM[:, 4:68]
        aneg = SM[:, 68:132]
        dvs = SM[:, 132:136]
        nws = SM[:, 136:152]
        cbs = SM[:, 152:168]
        gsum = SM[:, 168:184]
        ggr = SM[:, 184:248]
        m128 = SM[:, 248:264]
        smrB = SM[:, 264:265]
        wB = SM[:, 265:273]
        ecol = SM[:, 273:279]
        b1g = SM[:, 279:301]
        b2g = SM[:, 301:333]
        b2c = SM[:, 333:349]
        carry = SM[:, 349:413]
        qcolf = SM[:, 413:415]
        cB = SM[:, 415:417]
        iotaf = SM[:, 417:418]
        iota4f = SM[:, 418:419]
        idxsf = SM[:, 419:420]
        dxs = SM[:, 448:960]
        identf = SM[:, 960:1088]
        dtws = SR[0:64, 0:512]
        gws = SR[:, 512:608]
        idents = SR[:, 608:736]
        onest = SR[:, 736:737]
        m128r = SR[:, 737:753]
        qcoli = SI[:, 0:2]
        iotai = SI[:, 2:3]
        iota4i = SI[:, 3:4]
        idxg = SI[:, 4:20]
        idxy = SI[:, 20:36]
        idxm = SI[:, 36:52]
        idxtb = SI[:, 52:53]
        iota8s = R8[:, 0:8]
        ls8 = R8[:, 8:16]
        mx8 = R8[:, 16:24]
        eq8 = R8[:, 24:32]
        grow = R8[:, 32:40]
        gtmp = R8[:, 40:48]

        PS = [ctx.enter_context(nc.psum_tensor(f"ps{i}", [128, 512], F32)) for i in range(8)]

        LD = Sem(ctx.enter_context(nc.semaphore("LD")))
        GP = Sem(ctx.enter_context(nc.semaphore("GP")))
        SA = Sem(ctx.enter_context(nc.semaphore("SA")))
        SV = Sem(ctx.enter_context(nc.semaphore("SV")))
        TM = Sem(ctx.enter_context(nc.semaphore("TM")))
        GC = Sem(ctx.enter_context(nc.semaphore("GC")))
        LDR = [Sem(ctx.enter_context(nc.semaphore(f"LDr{i}"))) for i in range(QDEPTH)]
        GPR = [Sem(ctx.enter_context(nc.semaphore(f"GPr{i}"))) for i in range(QDEPTH)]
        CCS = Sem(ctx.enter_context(nc.semaphore("CCS")))

        block = ctx.enter_context(nc.Block())
        P = Plan({"sync": LD, "gpsimd": GP}, {"sync": LDR, "gpsimd": GPR},
                 {"vector": SV, "scalar": SA, "gpsimd": GC, "tensor": TM}, CCS)

        REGIONS = ((0, 32896, FRt), (32896, 33924, SQRt), (33928, 34956, RBRt),
                   (34960, 39936, WINRt), (39936, 41984, ZTRt))

        def fa(off, ln, parts=128, p0=0):
            for s, e2, t in REGIONS:
                if s <= off and off + ln <= e2:
                    return t[p0:p0 + parts, off - s:off - s + ln]
            raise AssertionError((off, ln))

        fr = fa

        def faf(off, ln, parts=128, p0=0):
            return fa(off, ln, parts, p0).bitcast(F32)

        def fo(t, off, ln, parts=128, p0=0):
            return t[p0:p0 + parts, off:off + ln]

        def ba(off, ln, parts=128, p0=0):
            assert 0 <= off and off + ln <= 16384, (off, ln)
            return BA[p0:p0 + parts, off:off + ln]

        vx = [P.dma("sync", LD, fa(kc * TC, TC), x_fm[kc * 128:(kc + 1) * 128, :])
              for kc in range(NKC)]
        for dst, src in ((dtbs, dtb), (alogs, alog), (dvs, dvec), (nws, nw), (cbs, cbv),
                         (iota8s, iota8)):
            P.dma("sync", LD, dst[:], src)
        P.dma("sync", LD, qcoli, qoff.to_broadcast((128, 2)))
        v_consts = P.dma("sync", LD, cB, cvals.to_broadcast((128, 2)))
        P.dma("gpsimd", GP, dtws, dtw)
        P.dma("gpsimd", GP, gws, gwv)
        P.dma("gpsimd", GP, idents, identi)
        P.dma("gpsimd", GP, identf, identi)
        v_gp0 = P.dma("gpsimd", GP, onest, onesf)
        P.op("gpsimd", GC, lambda e: e.iota(iotai, pattern=[[0, 1]], base=0, channel_multiplier=1))
        v_iota = P.op("gpsimd", GC, lambda e: e.iota(iota4i, pattern=[[0, 1]], base=0, channel_multiplier=4))

        P.wait("vector", LD, v_consts)
        P.wait("vector", GC, v_iota)
        P.op("vector", SV, lambda e: e.memset(carry, 0.0))
        P.op("vector", SV, lambda e: e.memset(grow, 0.0))
        P.op("vector", SV, lambda e: e.memset(onesb[:], 1.0))
        P.op("vector", SV, lambda e: e.tensor_copy(out=iotaf, in_=iotai))
        P.op("vector", SV, lambda e: e.tensor_copy(out=iota4f, in_=iota4i))
        P.op("vector", SV, lambda e: e.tensor_copy(out=qcolf, in_=qcoli))
        P.wait("scalar", LD, v_consts)
        v_expal = P.op("scalar", SA, lambda e: e.activation(out=aneg, in_=alogs, func=AF.Exp))
        P.wait("vector", SA, v_expal)
        v_aneg = P.op("vector", SV, lambda e: e.tensor_scalar(
            out=aneg, in0=aneg, scalar1=-1.0, scalar2=None, op0=ALU.mult))

        XHN = 0
        XZ0 = 16448
        SQ = (32896, 32896)
        RB = 33928
        WIN = (34960, 37008)
        ZT = (39936, 40960)
        chunks = ((0, HALO), (HALO, HALO + 512), (HALO + 512, TC))

        P.wait("tensor", GP, v_gp0)
        var_mm_v = [None] * NKC
        for kc in range(NKC):
            P.wait("scalar", LD, vx[kc])
            if kc >= 1:
                P.wait("scalar", TM, var_mm_v[kc - 1])
            v_sq = P.op("scalar", SA, lambda e, kc=kc: e.activation(
                out=fr(SQ[kc % 2], TC), in_=fa(XHN + kc * TC, TC), func=AF.Square))
            P.wait("tensor", SA, v_sq)
            for ci, (c0, c1) in enumerate(chunks):
                fn = lambda e, kc=kc, ci=ci, c0=c0, c1=c1: e.matmul(
                    out=PS[5 + ci][0:1, 0:(c1 - c0)], lhsT=onest,
                    rhs=fr(SQ[kc % 2] + c0, c1 - c0),
                    start=(kc == 0), stop=(kc == NKC - 1))
                if ci == 2:
                    var_mm_v[kc] = P.op("tensor", TM, fn)
                else:
                    P.op("tensor", None, fn)
        P.wait("scalar", TM, var_mm_v[NKC - 1])
        for ci, (c0, c1) in enumerate(chunks):
            P.op("scalar", SA, lambda e, ci=ci, c0=c0, c1=c1: e.activation(
                out=fa(RB + c0, c1 - c0, parts=1), in_=PS[5 + ci][0:1, 0:(c1 - c0)],
                func=AF.Ln, scale=1.0 / H, bias=cB[0:1, 0:1]))
            v_rr = P.op("scalar", SA, lambda e, c0=c0, c1=c1: e.activation(
                out=fa(RB + c0, c1 - c0, parts=1), in_=fa(RB + c0, c1 - c0, parts=1),
                func=AF.Exp, scale=-0.5))
        P.wait("sync", SA, v_rr)
        v_w = P.dma("sync", LD, rowb[0:1, 0:TC], fa(RB, TC, parts=1))
        P.wait("sync", LD, v_w)
        v_rstdB = P.dma("sync", LD, fa(RB, TC), rowb[0:1, 0:TC].to_broadcast((128, TC)))

        P.wait("vector", LD, v_rstdB)
        hn_v = [None] * NKC
        for kc in range(NKC):
            hn_v[kc] = P.op("vector", SV, lambda e, kc=kc: e.scalar_tensor_tensor(
                out=fr(XHN + kc * TC, TC), in0=fa(XHN + kc * TC, TC), scalar=nws[:, kc:kc + 1],
                in1=fa(RB, TC), op0=ALU.mult, op1=ALU.mult))
        v_hn_all = hn_v[-1]

        ip_load_v = [None] * 32
        ip_mm_v = [None] * 32
        ip_evac_v = [None] * 32
        z_dma_v = {}

        def ip_compute(oc):
            bk = 0 if oc % 2 == 0 else 3
            P.wait("tensor", GP, ip_load_v[oc])
            P.wait("tensor", SV, v_hn_all)
            if oc >= 2:
                P.wait("tensor", SA, ip_evac_v[oc - 2])
            wslot = WIN[oc % 2]
            cl = chunks if oc < 16 else chunks[1:]
            for ci, (c0, c1) in enumerate(cl):
                for kc in range(NKC):
                    fn = lambda e, kc=kc, bk=bk, ci=ci, c0=c0, c1=c1, wslot=wslot: e.matmul(
                        out=PS[bk + ci][:, 0:(c1 - c0)],
                        lhsT=fr(wslot + kc * 128, 128),
                        rhs=fr(XHN + kc * TC + c0, c1 - c0),
                        start=(kc == 0), stop=(kc == NKC - 1))
                    if ci == len(cl) - 1 and kc == NKC - 1:
                        ip_mm_v[oc] = P.op("tensor", TM, fn)
                    else:
                        P.op("tensor", None, fn)
            P.wait("scalar", TM, ip_mm_v[oc])
            if oc < 16:
                for ci, (c0, c1) in enumerate(cl):
                    ip_evac_v[oc] = P.op("scalar", SA, lambda e, oc=oc, bk=bk, ci=ci, c0=c0, c1=c1:
                                         e.activation(out=fr(XZ0 + oc * TC + c0, c1 - c0),
                                                      in_=PS[bk + ci][:, 0:(c1 - c0)], func=AF.Copy))
            else:
                zslot = ZT[oc % 2]
                if oc >= 18:
                    P.wait("scalar", LD, z_dma_v[oc - 2])
                for ci, (c0, c1) in enumerate(cl):
                    ip_evac_v[oc] = P.op("scalar", SA, lambda e, zslot=zslot, bk=bk, ci=ci, c0=c0, c1=c1:
                                         e.activation(out=fa(zslot + (c0 - HALO), c1 - c0),
                                                      in_=PS[bk + ci][:, 0:(c1 - c0)], func=AF.Copy))
                P.wait("sync", SA, ip_evac_v[oc])
                z_dma_v[oc] = P.dma("sync", LD, zdram[(oc - 16) * 128:(oc - 15) * 128, :],
                                    fa(zslot, TL))

        for i in range(34):
            if i >= 2:
                ip_compute(i - 2)
            if i < 32:
                if i >= 2:
                    P.wait("gpsimd", TM, ip_mm_v[i - 2])
                ip_load_v[i] = P.dma("gpsimd", GP, fr(WIN[i % 2], 2048), wi[i * 128:(i + 1) * 128, :])

        XCB = 0
        cv_steps = [(oc, k) for oc in range(16) for k in range(4)]
        cvk_mm = {}
        cv_evac_v = [None] * 16
        xc_dma_v = [None] * 16
        cv_load_v = {}
        cchunks = ((HALO, HALO + 512), (HALO + 512, TC))

        def cv_step_compute(j):
            oc, k = cv_steps[j]
            bk = 0 if oc % 2 == 0 else 2
            P.wait("tensor", GP, cv_load_v[(oc, k)])
            if k == 0:
                P.wait("tensor", SA, ip_evac_v[14])
                P.wait("tensor", SA, ip_evac_v[15])
                if oc >= 2:
                    P.wait("tensor", SA, cv_evac_v[oc - 2])
            wslot = WIN[j % 2]
            for ci, (c0, c1) in enumerate(cchunks):
                for kc in range(NKC):
                    fn = lambda e, k=k, kc=kc, bk=bk, ci=ci, c0=c0, wslot=wslot: e.matmul(
                        out=PS[bk + ci][:, 0:512],
                        lhsT=fr(wslot + kc * 128, 128),
                        rhs=fr(XZ0 + kc * TC + (c0 - HALO) + k + 1, 512),
                        start=(k == 0 and kc == 0), stop=(k == 3 and kc == NKC - 1))
                    if ci == 1 and kc == NKC - 1:
                        cvk_mm[(oc, k)] = P.op("tensor", TM, fn)
                    else:
                        P.op("tensor", None, fn)
            if k == 3:
                P.wait("scalar", TM, cvk_mm[(oc, 3)])
                P.wait("scalar", TM, ip_mm_v[31])
                for ci in range(2):
                    cv_evac_v[oc] = P.op("scalar", SA, lambda e, oc=oc, bk=bk, ci=ci: e.activation(
                        out=fr(XCB + oc * TL + ci * 512, 512), in_=PS[bk + ci][:, 0:512],
                        func=AF.Silu, bias=cbs[:, oc:oc + 1]))
                P.wait("sync", SA, cv_evac_v[oc])
                xc_dma_v[oc] = P.dma("sync", LD, xcst[oc * 128:(oc + 1) * 128, :],
                                     faf(XCB + oc * TL, TL))

        for j in range(len(cv_steps) + 2):
            if j >= 2:
                cv_step_compute(j - 2)
            if j < len(cv_steps):
                oc, k = cv_steps[j]
                if j < 2:
                    P.wait("gpsimd", TM, ip_mm_v[31])
                    P.wait("gpsimd", LD, z_dma_v[31])
                else:
                    P.wait("gpsimd", TM, cvk_mm[cv_steps[j - 2]])
                cv_load_v[(oc, k)] = P.dma("gpsimd", GP, fr(WIN[j % 2], 2048),
                                           wc[oc * 128:(oc + 1) * 128, k * 2048:(k + 1) * 2048])

        xpw = XPWr[:, 0:1536]
        P.wait("gpsimd", LD, z_dma_v[31])
        v_xpw = P.dma("gpsimd", GP, xpw, xpv)
        DBC = SQ[0]
        xp_mm_v = [None] * 2
        v_dbc = None
        P.wait("tensor", SA, cv_evac_v[15])
        P.wait("tensor", SA, ip_evac_v[31])
        P.wait("tensor", GP, v_xpw)
        for tci in range(2):
            for kc in range(NKC):
                fn = lambda e, kc=kc, tci=tci: e.matmul(
                    out=PS[4 + tci][0:96, 0:512],
                    lhsT=xpw[:, kc * 96:(kc + 1) * 96],
                    rhs=fr(XCB + kc * TL + tci * 512, 512),
                    start=(kc == 0), stop=(kc == NKC - 1))
                if kc == NKC - 1:
                    xp_mm_v[tci] = P.op("tensor", TM, fn)
                else:
                    P.op("tensor", None, fn)
            P.wait("vector", TM, xp_mm_v[tci])
            v_dbc = P.op("vector", SV, lambda e, tci=tci: e.tensor_copy(
                out=fa(DBC + tci * 512, 512, parts=96), in_=PS[4 + tci][0:96, 0:512]))
        P.wait("sync", SV, v_dbc)
        v_dbcst = P.dma("sync", LD, dbcst, faf(DBC, TL, parts=96))

        P.wait("gpsimd", LD, max(xc_dma_v[15], v_dbcst))
        v_cc_xc = P.cc("gpsimd", GP, "AllGather", xcst, xcg)
        v_cc_dbc = P.cc("gpsimd", GP, "AllGather", dbcst, dbcg)

        DEL2 = 0
        U2 = 4096
        XCC = 8192
        DPRE = 12288
        SCT = 19456
        BMB = 23552
        CMB = 25600
        YSC = 27648
        SPS = 28672
        BMS = 27648

        P.wait("gpsimd", GP, v_cc_dbc)
        v_dpre = None
        v_bms = None
        for r in range(4):
            v_dpre = P.dma("gpsimd", GP, fr(DPRE + r * TL, TL, parts=64),
                           dbcg[r * 96:r * 96 + 64, :])
            v_bms = P.dma("gpsimd", GP, fr(BMS + r * TL, TL, parts=32),
                          dbcg[r * 96 + 64:r * 96 + 96, :])
        P.wait("sync", GP, v_bms)
        v_bmcmd = P.dma("sync", LD, bmcmd, fa(BMS, L, parts=32))
        P.wait("vector", LD, v_bmcmd)
        P.wait("scalar", LD, v_bmcmd)

        v_idx = None
        for r in range(4):
            for hb in range(4):
                j = r * 4 + hb
                P.op("vector", SV, lambda e, r=r, hb=hb: e.tensor_scalar(
                    out=idxsf, in0=iotaf, scalar1=qcolf[:, 0:1],
                    scalar2=float(r * 2048 + hb * 128), op0=ALU.add, op1=ALU.add))
                v_idx = P.op("vector", SV, lambda e, j=j: e.tensor_copy(
                    out=idxg[:, j:j + 1], in_=idxsf))
        v_idxy = None
        for kc in range(NKC):
            P.op("vector", SV, lambda e, kc=kc: e.tensor_scalar(
                out=idxsf, in0=iota4f, scalar1=qcolf[:, 1:2],
                scalar2=float(kc * 512), op0=ALU.add, op1=ALU.add))
            v_idxy = P.op("vector", SV, lambda e, kc=kc: e.tensor_copy(
                out=idxy[:, kc:kc + 1], in_=idxsf))
        P.wait("gpsimd", SV, v_idx)
        P.wait("gpsimd", GP, v_cc_xc)

        NIT = 128
        bm_ld_v = [None] * NIT
        xcc_v = {}
        u_v = {}
        a_v = {}
        bq_v = {}
        tts_v = {}
        sc_v = {}
        mm_v = {}
        yev_v = {}
        ysd_v = {}
        dt_relu = {}

        def emit_tc_prep(tc):
            par = tc % 2
            r = tc // 2
            ybk = (tc % 2) * 4
            for hb in range(4):
                if tc >= 2:
                    P.wait("gpsimd", SV, yev_v[(tc - 2, hb)])
                xcc_v[(tc, hb)] = P.idma(
                    "gpsimd", GP, fr(XCC + par * 2048 + hb * 512, 512), xcg,
                    idxg[:, (r * 4 + hb):(r * 4 + hb) + 1],
                    element_offset=(tc % 2) * 512)
                bk = ybk + hb
                P.wait("tensor", GP, v_dpre)
                if tc >= 2:
                    P.wait("tensor", SV, yev_v[(tc - 2, hb)])
                v_dtmm = P.op("tensor", TM, lambda e, hb=hb, tc=tc, bk=bk: e.matmul(
                    out=PS[bk][:, 0:512], lhsT=dtws[:, hb * 128:(hb + 1) * 128],
                    rhs=fr(DPRE + tc * 512, 512, parts=64), start=True, stop=True))
                m = tc * 4 + hb
                P.wait("scalar", TM, v_dtmm)
                if m >= 2:
                    ptc, phb = divmod(m - 2, 4)
                    P.wait("scalar", SV, u_v[(ptc, phb)])
                sps = SPS + (m % 2) * 512
                dslot = DEL2 + par * 2048 + hb * 512
                P.op("scalar", SA, lambda e, bk=bk, sps=sps, hb=hb: e.activation(
                    out=fa(sps, 512), in_=PS[bk][:, 0:512], func=AF.Abs, bias=dtbs[:, hb:hb + 1]))
                P.op("scalar", SA, lambda e, sps=sps: e.activation(
                    out=fa(sps, 512), in_=fa(sps, 512), func=AF.Exp, scale=-1.0))
                P.op("scalar", SA, lambda e, sps=sps: e.activation(
                    out=fa(sps, 512), in_=fa(sps, 512), func=AF.Ln, bias=1.0))
                v_relu = P.op("scalar", SA, lambda e, bk=bk, hb=hb, dslot=dslot: e.activation(
                    out=fa(dslot, 512), in_=PS[bk][:, 0:512], func=AF.Relu, bias=dtbs[:, hb:hb + 1]))
                dt_relu[(tc, hb)] = v_relu
                if tc >= 2:
                    P.wait("vector", SA, a_v[(tc - 2, 15, hb)])
                P.wait("vector", SA, v_relu)
                P.op("vector", SV, lambda e, dslot=dslot, sps=sps: e.tensor_add(
                    out=fa(dslot, 512), in0=fa(dslot, 512), in1=fa(sps, 512)))
                P.wait("vector", GP, xcc_v[(tc, hb)])
                if tc >= 2 and hb >= 2:
                    P.wait("vector", GC, bq_v[(16 * (tc - 2) + 15, hb)])
                u_v[(tc, hb)] = P.op("vector", SV, lambda e, dslot=dslot, par=par, hb=hb: e.tensor_mul(
                    out=fa(U2 + par * 2048 + hb * 512, 512), in0=fa(dslot, 512),
                    in1=fa(XCC + par * 2048 + hb * 512, 512)))

        def emit_bm_load(i):
            tc, n = divmod(i, 16)
            par = i % 4
            if i >= 4:
                P.wait("sync", GC, sc_v[(i - 4, 3)])
            P.wait("sync", LD, v_bmcmd)
            P.dma("sync", LD, fa(BMB + par * 512, 512),
                  bmcmd[n:n + 1, tc * 512:(tc + 1) * 512].to_broadcast((128, 512)))
            bm_ld_v[i] = P.dma("sync", LD, fa(CMB + par * 512, 512),
                               bmcmd[16 + n:17 + n, tc * 512:(tc + 1) * 512].to_broadcast((128, 512)))

        def emit_scan_iter(i):
            tc, n = divmod(i, 16)
            par = i % 2
            par4 = i % 4
            tpar = tc % 2
            ybk = (tc % 2) * 4
            for hb in range(4):
                cidx = hb * 16 + n
                g = i * 4 + hb
                aslot = (g % 4) * 512
                dslot = DEL2 + tpar * 2048 + hb * 512
                if g >= 4:
                    pi, phb = divmod(g - 4, 4)
                    P.wait("scalar", SV, tts_v[(pi, phb)])
                P.wait("scalar", SV, u_v[(tc, hb)])
                P.wait("scalar", SV, v_aneg)
                a_v[(tc, n, hb)] = P.op("scalar", SA, lambda e, hb=hb, n=n, aslot=aslot, dslot=dslot:
                                        e.activation(out=fo(ABf, aslot, 512), in_=fa(dslot, 512),
                                                     func=AF.Exp,
                                                     scale=aneg[:, hb * 16 + n:hb * 16 + n + 1]))
                bslot = (g % 2) * 512
                sslot = (g % 4) * 512
                if hb >= 2:
                    P.wait("gpsimd", LD, bm_ld_v[i])
                    P.wait("gpsimd", SV, u_v[(tc, hb)])
                    if g >= 2:
                        pg = g - 2
                        P.wait("gpsimd", SV, tts_v[(pg // 4, pg % 4)])
                    bq_v[(i, hb)] = P.op("gpsimd", GC, lambda e, hb=hb, tpar=tpar, par4=par4, bslot=bslot:
                                         e.tensor_tensor(
                                             out=fo(BTf, bslot, 512),
                                             in0=fa(U2 + tpar * 2048 + hb * 512, 512),
                                             in1=fa(BMB + par4 * 512, 512), op=ALU.mult))
                else:
                    P.wait("vector", LD, bm_ld_v[i])
                    bq_v[(i, hb)] = P.op("vector", SV, lambda e, hb=hb, tpar=tpar, par4=par4, bslot=bslot:
                                         e.tensor_tensor(
                                             out=fo(BTf, bslot, 512),
                                             in0=fa(U2 + tpar * 2048 + hb * 512, 512),
                                             in1=fa(BMB + par4 * 512, 512), op=ALU.mult))
                P.wait("vector", SA, a_v[(tc, n, hb)])
                if hb >= 2:
                    P.wait("vector", GC, bq_v[(i, hb)])
                if g >= 4:
                    P.wait("vector", GC, sc_v[((g - 4) // 4, (g - 4) % 4)])
                tts_v[(i, hb)] = P.op("vector", SV, lambda e, sslot=sslot, aslot=aslot, cidx=cidx, bslot=bslot:
                                      e.tensor_tensor_scan(out=fo(STf, sslot, 512),
                                                           data0=fo(ABf, aslot, 512),
                                                           data1=fo(BTf, bslot, 512),
                                                           initial=carry[:, cidx:cidx + 1],
                                                           op0=ALU.mult, op1=ALU.add))
                P.op("vector", SV, lambda e, sslot=sslot, cidx=cidx: e.tensor_copy(
                    out=carry[:, cidx:cidx + 1], in_=fo(STf, sslot + 511, 1)))
                scs = (hb * 2 + par) * 512
                P.wait("gpsimd", SV, tts_v[(i, hb)])
                if i >= 2:
                    P.wait("gpsimd", TM, mm_v[(i - 2, hb)])
                sc_v[(i, hb)] = P.op("gpsimd", GC, lambda e, sslot=sslot, par4=par4, scs=scs: e.tensor_mul(
                    out=fo(SCf, scs, 512), in0=fo(STf, sslot, 512),
                    in1=fa(CMB + par4 * 512, 512)))
                P.wait("tensor", GC, sc_v[(i, hb)])
                mm_v[(i, hb)] = P.op("tensor", TM, lambda e, scs=scs, ybk=ybk, hb=hb, n=n: e.matmul(
                    out=PS[ybk + hb][:, 0:512], lhsT=identf, rhs=fo(SCf, scs, 512),
                    start=(n == 0), stop=(n == 15)))

        def emit_y_evac(tc):
            ybk = (tc % 2) * 4
            tpar = tc % 2
            for hb in range(4):
                P.wait("vector", TM, mm_v[(tc * 16 + 15, hb)])
                j = tc * 4 + hb
                ysl = YSC + (j % 2) * 512
                if j >= 2:
                    P.wait("vector", LD, ysd_v[j - 2])
                P.op("vector", SV, lambda e, hb=hb, tpar=tpar: e.tensor_scalar(
                    out=dxs[:], in0=fa(XCC + tpar * 2048 + hb * 512, 512),
                    scalar1=dvs[:, hb:hb + 1], scalar2=None, op0=ALU.mult))
                yev_v[(tc, hb)] = P.op("vector", SV, lambda e, ybk=ybk, hb=hb, ysl=ysl: e.tensor_add(
                    out=fa(ysl, 512), in0=PS[ybk + hb][:, 0:512], in1=dxs[:]))
                P.wait("sync", SV, yev_v[(tc, hb)])
                ysd_v[j] = P.dma("sync", LD, yst[hb * 128:(hb + 1) * 128, tc * 512:(tc + 1) * 512],
                                 faf(ysl, 512))

        emit_tc_prep(0)
        for i in range(3):
            emit_bm_load(i)
        for i in range(NIT):
            tc, n = divmod(i, 16)
            if n == 0 and tc + 1 < 8:
                emit_tc_prep(tc + 1)
            if i + 3 < NIT:
                emit_bm_load(i + 3)
            emit_scan_iter(i)
            if n == 15:
                emit_y_evac(tc)

        P.wait("gpsimd", LD, ysd_v[31])
        v_cc_y = P.cc("gpsimd", GP, "AllGather", yst, yg)

        XR = 0
        V3 = 16384
        RB1 = 32768
        MRB = 33792
        T1 = 34816
        WOS = (35840, 37888)
        ZYT = (39936, 40960)
        OUTS = (39936, 40960)
        SQ3 = WOS[0]
        G3B = 0
        XR2B = 16384
        W1EB = 34816

        vv_v = [None] * NKC
        sz_done = [None] * NKC
        P.wait("sync", LD, ysd_v[31])
        P.wait("vector", LD, bm_ld_v[NIT - 1])
        P.wait("gpsimd", GP, v_cc_y)
        P.wait("gpsimd", SV, v_idxy)
        for kc in range(NKC):
            zslot, yslot = ZYT
            if kc >= 1:
                P.wait("sync", SA, sz_done[kc - 1])
                P.wait("sync", SV, vv_v[kc - 1])
            v_z = P.dma("sync", LD, fa(zslot, TL), zdram[kc * 128:(kc + 1) * 128, :])
            if kc >= 1:
                P.wait("gpsimd", SV, vv_v[kc - 1])
            v_y = P.idma("gpsimd", GP, fa(yslot, TL), ygv, idxy[:, kc:kc + 1])
            P.wait("scalar", LD, v_z)
            sz_done[kc] = P.op("scalar", SA, lambda e, zslot=zslot: e.activation(
                out=fa(zslot, TL), in_=fa(zslot, TL), func=AF.Silu))
            P.wait("vector", SA, sz_done[kc])
            P.wait("vector", GP, v_y)
            vv_v[kc] = P.op("vector", SV, lambda e, kc=kc, zslot=zslot, yslot=yslot: e.tensor_mul(
                out=fr(V3 + kc * TL, TL), in0=fa(yslot, TL), in1=fa(zslot, TL)))

        xr_v = [P.dma("sync", LD, fa(XR + kc * TL, TL), x_fm[kc * 128:(kc + 1) * 128, HALO:TC])
                for kc in range(NKC)]

        op_load_v = [None] * NKC
        op_mm_v = [None] * NKC
        op_evac_v = [None] * NKC

        def op_compute(oc):
            bk = 0 if oc % 2 == 0 else 2
            P.wait("tensor", GP, op_load_v[oc])
            P.wait("tensor", SV, vv_v[NKC - 1])
            if oc >= 2:
                P.wait("tensor", SV, op_evac_v[oc - 2])
            for tci in range(2):
                for kc in range(NKC):
                    fn = lambda e, oc=oc, tci=tci, kc=kc, bk=bk: e.matmul(
                        out=PS[bk + tci][:, 0:512],
                        lhsT=fr(WOS[oc % 2] + kc * 128, 128),
                        rhs=fr(V3 + kc * TL + tci * 512, 512),
                        start=(kc == 0), stop=(kc == NKC - 1))
                    if tci == 1 and kc == NKC - 1:
                        op_mm_v[oc] = P.op("tensor", TM, fn)
                    else:
                        P.op("tensor", None, fn)
            P.wait("vector", TM, op_mm_v[oc])
            P.wait("vector", LD, xr_v[oc])
            for tci in range(2):
                op_evac_v[oc] = P.op("vector", SV, lambda e, oc=oc, tci=tci, bk=bk:
                                     e.scalar_tensor_tensor(
                                         out=fr(XR + oc * TL + tci * 512, 512),
                                         in0=fa(XR + oc * TL + tci * 512, 512), scalar=2.0,
                                         in1=PS[bk + tci][:, 0:512],
                                         op0=ALU.mult, op1=ALU.add))

        for i in range(18):
            if i >= 2:
                op_compute(i - 2)
            if i < 16:
                if i >= 2:
                    P.wait("gpsimd", TM, op_mm_v[i - 2])
                else:
                    P.wait("gpsimd", GC, sc_v[(NIT - 1, 3)])
                op_load_v[i] = P.dma("gpsimd", GP, fr(WOS[i % 2], 2048), wo[i * 128:(i + 1) * 128, :])

        mmq = {}
        v_mr = None
        v_rowread = None
        for tci in range(2):
            if v_rowread is not None:
                P.wait("tensor", SV, v_rowread)
            for kc in range(NKC):
                P.wait("tensor", SV, op_evac_v[kc])
                fn = lambda e, kc=kc, tci=tci: e.matmul(
                    out=PS[6][0:1, 0:512], lhsT=onest, rhs=fr(XR + kc * TL + tci * 512, 512),
                    start=(kc == 0), stop=(kc == NKC - 1))
                if kc == NKC - 1:
                    v_sum = P.op("tensor", TM, fn)
                else:
                    P.op("tensor", None, fn)
            for kc in range(NKC):
                P.wait("scalar", SV, op_evac_v[kc])
                P.wait("scalar", TM, op_mm_v[NKC - 1])
                P.wait("scalar", GP, op_load_v[NKC - 1])
                if kc >= 1:
                    P.wait("scalar", TM, mmq[(tci, kc - 1)])
                v_sq = P.op("scalar", SA, lambda e, kc=kc, tci=tci: e.activation(
                    out=fr(SQ3, 512), in_=fa(XR + kc * TL + tci * 512, 512), func=AF.Square))
                P.wait("tensor", SA, v_sq)
                fn = lambda e, kc=kc: e.matmul(
                    out=PS[7][0:1, 0:512], lhsT=onest, rhs=fr(SQ3, 512),
                    start=(kc == 0), stop=(kc == NKC - 1))
                mmq[(tci, kc)] = P.op("tensor", TM, fn)
            P.wait("vector", TM, mmq[(tci, NKC - 1)])
            P.wait("vector", TM, v_sum)
            P.op("vector", SV, lambda e, tci=tci: e.tensor_scalar(
                out=fo(MRBf, tci * 512, 512, parts=1), in0=PS[6][0:1, 0:512],
                scalar1=1.0 / H, scalar2=None, op0=ALU.mult))
            v_rowread = P.op("vector", SV, lambda e, tci=tci: e.tensor_scalar(
                out=fo(RB1f, tci * 512, 512, parts=1), in0=PS[7][0:1, 0:512],
                scalar1=1.0 / H, scalar2=None, op0=ALU.mult))
            P.op("vector", SV, lambda e, tci=tci: e.tensor_mul(
                out=fo(T1f, tci * 512, 512, parts=1), in0=fo(MRBf, tci * 512, 512, parts=1),
                in1=fo(MRBf, tci * 512, 512, parts=1)))
            v_var = P.op("vector", SV, lambda e, tci=tci: e.tensor_sub(
                out=fo(RB1f, tci * 512, 512, parts=1), in0=fo(RB1f, tci * 512, 512, parts=1),
                in1=fo(T1f, tci * 512, 512, parts=1)))
            P.wait("scalar", SV, v_var)
            P.op("scalar", SA, lambda e, tci=tci: e.activation(
                out=fo(RB1f, tci * 512, 512, parts=1), in_=fo(RB1f, tci * 512, 512, parts=1),
                func=AF.Ln, bias=cB[0:1, 0:1]))
            v_rs = P.op("scalar", SA, lambda e, tci=tci: e.activation(
                out=fo(RB1f, tci * 512, 512, parts=1), in_=fo(RB1f, tci * 512, 512, parts=1),
                func=AF.Exp, scale=-0.5))
            P.wait("vector", SA, v_rs)
            v_mr = P.op("vector", SV, lambda e, tci=tci: e.tensor_mul(
                out=fo(MRBf, tci * 512, 512, parts=1), in0=fo(MRBf, tci * 512, 512, parts=1),
                in1=fo(RB1f, tci * 512, 512, parts=1)))
        v_smr = P.op("vector", SV, lambda e: e.tensor_reduce(
            out=fo(T1f, 0, 1, parts=1), in_=fo(MRBf, 0, TL, parts=1), axis=mybir.AxisListType.X, op=ALU.add))
        P.wait("sync", SV, v_smr)
        P.dma("sync", LD, rowb[0:1, 0:TL], fo(RB1f, 0, TL, parts=1))
        P.dma("sync", LD, rowb[0:1, TL:2 * TL], fo(MRBf, 0, TL, parts=1))
        v_w = P.dma("sync", LD, rowb[0:1, 2 * TL:2 * TL + 1], fo(T1f, 0, 1, parts=1))
        P.wait("sync", LD, v_w)
        P.dma("sync", LD, fo(RB1f, 0, TL), rowb[0:1, 0:TL].to_broadcast((128, TL)))
        P.dma("sync", LD, smrB, rowb[0:1, 2 * TL:2 * TL + 1].to_broadcast((128, 1)))
        v_bc = P.dma("sync", LD, fo(MRBf, 0, TL), rowb[0:1, TL:2 * TL].to_broadcast((128, TL)))
        P.wait("vector", LD, v_bc)
        xln_v = [None] * NKC
        for kc in range(NKC):
            P.op("vector", SV, lambda e, kc=kc: e.tensor_mul(
                out=fo(T1f, 0, TL), in0=fa(XR + kc * TL, TL), in1=fo(RB1f, 0, TL)))
            P.op("vector", SV, lambda e, kc=kc: e.tensor_reduce(
                out=gsum[:, kc:kc + 1], in_=fo(T1f, 0, TL), axis=mybir.AxisListType.X, op=ALU.add))
            xln_v[kc] = P.op("vector", SV, lambda e, kc=kc: e.tensor_sub(
                out=ba(kc * TL, TL), in0=fo(T1f, 0, TL), in1=fo(MRBf, 0, TL)))
        v_gl = P.op("vector", SV, lambda e: e.tensor_scalar(
            out=gsum, in0=gsum, scalar1=smrB[:, 0:1], scalar2=None, op0=ALU.subtract))
        P.wait("sync", SV, v_gl)
        v_gst = P.dma("sync", LD, gst, gsum)
        P.wait("gpsimd", LD, v_gst)
        v_cc_g = P.cc("gpsimd", GP, "AllGather", gst, gg)
        P.wait("sync", GP, v_cc_g)
        v_ggr = None
        for r in range(4):
            v_ggr = P.dma("sync", LD, ggr[:, r * NKC:(r + 1) * NKC], gg[r * 128:(r + 1) * 128, :])
        P.wait("vector", LD, v_ggr)
        P.op("vector", SV, lambda e: e.tensor_add(out=m128, in0=ggr[:, 0:NKC], in1=ggr[:, NKC:2 * NKC]))
        P.op("vector", SV, lambda e: e.tensor_add(out=m128, in0=m128, in1=ggr[:, 2 * NKC:3 * NKC]))
        P.op("vector", SV, lambda e: e.tensor_add(out=m128, in0=m128, in1=ggr[:, 3 * NKC:4 * NKC]))
        v_m128 = P.op("vector", SV, lambda e: e.tensor_scalar(
            out=m128r, in0=m128, scalar1=1.0 / L, scalar2=None, op0=ALU.mult))
        P.wait("tensor", SV, v_m128)
        for kc in range(NKC):
            fn = lambda e, kc=kc: e.matmul(
                out=PS[4][0:1, 0:6], lhsT=m128r[:, kc:kc + 1], rhs=gws[:, kc * 6:(kc + 1) * 6],
                start=(kc == 0), stop=(kc == NKC - 1))
            if kc == NKC - 1:
                v_glog = P.op("tensor", TM, fn)
            else:
                P.op("tensor", None, fn)
        P.wait("vector", TM, v_glog)
        P.op("vector", SV, lambda e: e.memset(ls8, -3.0e38))
        P.op("vector", SV, lambda e: e.tensor_copy(out=ls8[:, 0:6], in_=PS[4][0:1, 0:6]))
        P.op("vector", SV, lambda e: e.max(out=mx8, in_=ls8))
        P.op("vector", SV, lambda e: e.tensor_scalar(
            out=eq8, in0=ls8, scalar1=mx8[:, 0:1], scalar2=None, op0=ALU.is_equal))
        P.op("vector", SV, lambda e: e.tensor_mul(out=eq8, in0=eq8, in1=iota8s))
        P.op("vector", SV, lambda e: e.tensor_reduce(
            out=grow[:, 2:3], in_=eq8, axis=mybir.AxisListType.X, op=ALU.add))
        P.op("vector", SV, lambda e: e.tensor_scalar(
            out=eq8, in0=ls8, scalar1=mx8[:, 1:2], scalar2=None, op0=ALU.is_equal))
        P.op("vector", SV, lambda e: e.tensor_mul(out=eq8, in0=eq8, in1=iota8s))
        P.op("vector", SV, lambda e: e.tensor_reduce(
            out=grow[:, 3:4], in_=eq8, axis=mybir.AxisListType.X, op=ALU.add))
        v_d01 = P.op("vector", SV, lambda e: e.tensor_sub(out=gtmp[:, 0:1], in0=mx8[:, 0:1], in1=mx8[:, 1:2]))
        P.wait("scalar", SV, v_d01)
        v_w0 = P.op("scalar", SA, lambda e: e.activation(out=grow[:, 0:1], in_=gtmp[:, 0:1], func=AF.Sigmoid))
        P.wait("vector", SA, v_w0)
        v_grow = P.op("vector", SV, lambda e: e.tensor_scalar(
            out=grow[:, 1:2], in0=grow[:, 0:1], scalar1=-1.0, scalar2=1.0, op0=ALU.mult, op1=ALU.add))
        P.wait("sync", SV, v_grow)
        v_w = P.dma("sync", LD, wrow, grow)
        P.wait("sync", LD, v_w)
        v_wB = P.dma("sync", LD, wB, wrow.to_broadcast((128, 8)))
        P.wait("vector", LD, v_wB)
        v_ecol = None
        for j, (src, mul) in enumerate(((2, 2048.0), (2, 1408.0), (2, 128.0),
                                        (3, 2048.0), (3, 1408.0), (3, 128.0))):
            v_ecol = P.op("vector", SV, lambda e, j=j, src=src, mul=mul: e.tensor_scalar(
                out=ecol[:, j:j + 1], in0=wB[:, src:src + 1], scalar1=mul, scalar2=None, op0=ALU.mult))

        _bg_prev = [None]

        def bias_gather(col, table, dst_ap):
            if _bg_prev[0] is not None:
                P.wait("vector", GP, _bg_prev[0])
            P.op("vector", SV, lambda e, col=col: e.tensor_add(
                out=idxsf, in0=iotaf, in1=ecol[:, col:col + 1]))
            v2 = P.op("vector", SV, lambda e: e.tensor_copy(out=idxtb, in_=idxsf))
            P.wait("gpsimd", SV, v2)
            _bg_prev[0] = P.idma("gpsimd", GP, dst_ap, table, idxtb)
            return _bg_prev[0]

        P.wait("vector", SV, v_ecol)
        bias_gather(2, b1fm, b1g[:, 0:NFC])
        bias_gather(5, b1fm, b1g[:, NFC:2 * NFC])
        bias_gather(2, b2fm, b2g[:, 0:NKC])
        v_bg = bias_gather(5, b2fm, b2g[:, NKC:2 * NKC])
        P.wait("vector", GP, v_bg)
        P.op("vector", SV, lambda e: e.tensor_scalar(
            out=b2c, in0=b2g[:, 0:NKC], scalar1=wB[:, 0:1], scalar2=None, op0=ALU.mult))
        v_b2c = P.op("vector", SV, lambda e: e.scalar_tensor_tensor(
            out=b2c, in0=b2g[:, NKC:2 * NKC], scalar=wB[:, 1:2], in1=b2c,
            op0=ALU.mult, op1=ALU.add))

        moe_gv = []

        def moe_idx(col, addi):
            P.op("vector", SV, lambda e, col=col: e.tensor_add(
                out=idxsf, in0=iotaf, in1=ecol[:, col:col + 1]))
            if addi:
                P.op("vector", SV, lambda e, addi=addi: e.tensor_scalar(
                    out=idxsf, in0=idxsf, scalar1=float(addi), scalar2=None, op0=ALU.add))
            slot = moe_idx.c % 16
            if moe_idx.c >= 16:
                P.wait("vector", GP, moe_gv[moe_idx.c - 16])
            moe_idx.c += 1
            v = P.op("vector", SV, lambda e, slot=slot: e.tensor_copy(
                out=idxm[:, slot:slot + 1], in_=idxsf))
            return v, idxm[:, slot:slot + 1]
        moe_idx.c = 0

        ff_evac = {}
        gel_all = {}
        gel_scaled_v = {}
        h1_last_mm = [None, None]
        ff_last_mm = [None, None]

        for s in range(2):
            ec0 = 0 if s == 0 else 3
            v_w1g = None
            for kc in range(NKC):
                vi, iap = moe_idx(ec0 + 0, kc * 128)
                P.wait("gpsimd", SV, vi)
                if s == 0:
                    P.wait("gpsimd", TM, op_mm_v[NKC - 1])
                else:
                    P.wait("gpsimd", TM, ff_last_mm[0])
                v_w1g = P.idma("gpsimd", GP, fo(W1B, kc * DFF, DFF), w1f, iap)
                moe_gv.append(v_w1g)
            P.wait("tensor", GP, v_w1g)
            P.wait("tensor", SV, xln_v[NKC - 1])
            if s == 0:
                P.wait("tensor", SV, op_evac_v[NKC - 1])
            for fc in range(NFC):
                for tci in range(2):
                    g = fc * 2 + tci
                    bk = g % 4
                    if g >= 4:
                        P.wait("tensor", SA, gel_all[(s, g - 4)])
                    elif s == 1:
                        P.wait("tensor", SA, gel_all[(0, NFC * 2 - 4 + g)])
                    for kc in range(NKC):
                        fn = lambda e, fc=fc, tci=tci, kc=kc, bk=bk: e.matmul(
                            out=PS[bk][:, 0:512],
                            lhsT=fo(W1B, kc * DFF + fc * 128, 128),
                            rhs=ba(kc * TL + tci * 512, 512),
                            start=(kc == 0), stop=(kc == NKC - 1))
                        if kc == NKC - 1:
                            v_h1 = P.op("tensor", TM, fn)
                        else:
                            P.op("tensor", None, fn)
                    if fc == NFC - 1 and tci == 1:
                        h1_last_mm[s] = v_h1
                    P.wait("scalar", TM, v_h1)
                    if s == 1 and g < 4:
                        P.wait("scalar", TM, ff_last_mm[0])
                    gel_all[(s, g)] = P.op("scalar", SA, lambda e, fc=fc, tci=tci, bk=bk, s=s: e.activation(
                        out=fo(GB, fc * TL + tci * 512, 512), in_=PS[bk][:, 0:512],
                        func=AF.Gelu, bias=b1g[:, s * NFC + fc:s * NFC + fc + 1]))
                    P.wait("vector", SA, gel_all[(s, g)])
                    gel_scaled_v[(s, g)] = P.op("vector", SV, lambda e, fc=fc, tci=tci, s=s: e.tensor_scalar(
                        out=fo(GB, fc * TL + tci * 512, 512), in0=fo(GB, fc * TL + tci * 512, 512),
                        scalar1=wB[:, s:s + 1], scalar2=None, op0=ALU.mult))
            v_w2g = None
            for fc in range(NFC):
                vi, iap = moe_idx(ec0 + 1, fc * 128)
                P.wait("gpsimd", SV, vi)
                P.wait("gpsimd", TM, h1_last_mm[s])
                v_w2g = P.idma("gpsimd", GP, fo(W1B, fc * H, H), w2f, iap)
                moe_gv.append(v_w2g)
            P.wait("tensor", GP, v_w2g)
            P.wait("vector", SV, v_b2c)
            for oc in range(NKC):
                for tci in range(2):
                    g = oc * 2 + tci
                    bk = 4 + g % 4
                    if g >= 4:
                        P.wait("tensor", SV, ff_evac[(s, g - 4)])
                    elif s == 1:
                        P.wait("tensor", SV, ff_evac[(0, NKC * 2 - 4 + g)])
                    for fc in range(NFC):
                        P.wait("tensor", SV, gel_scaled_v[(s, fc * 2 + tci)])
                        fn = lambda e, oc=oc, tci=tci, fc=fc, bk=bk: e.matmul(
                            out=PS[bk][:, 0:512],
                            lhsT=fo(W1B, fc * H + oc * 128, 128),
                            rhs=fo(GB, fc * TL + tci * 512, 512),
                            start=(fc == 0), stop=(fc == NFC - 1))
                        if fc == NFC - 1:
                            v_ff = P.op("tensor", TM, fn)
                        else:
                            P.op("tensor", None, fn)
                    if oc == NKC - 1 and tci == 1:
                        ff_last_mm[s] = v_ff
                    P.wait("vector", TM, v_ff)
                    if s == 0:
                        ff_evac[(s, g)] = P.op("vector", SV, lambda e, oc=oc, tci=tci, bk=bk:
                                               e.scalar_tensor_tensor(
                                                   out=fo(X2B, oc * TL + tci * 512, 512),
                                                   in0=PS[bk][:, 0:512], scalar=b2c[:, oc:oc + 1],
                                                   in1=ba(oc * TL + tci * 512, 512),
                                                   op0=ALU.add, op1=ALU.add))
                    else:
                        ff_evac[(s, g)] = P.op("vector", SV, lambda e, oc=oc, tci=tci, bk=bk:
                                               e.tensor_add(
                                                   out=fo(X2B, oc * TL + tci * 512, 512),
                                                   in0=PS[bk][:, 0:512],
                                                   in1=fo(X2B, oc * TL + tci * 512, 512)))

        P.wait("tensor", SV, ff_evac[(1, NKC * 2 - 1)])
        mmq2 = {}
        v_mr2 = None
        v_rowread2 = None
        for tci in range(2):
            if v_rowread2 is not None:
                P.wait("tensor", SV, v_rowread2)
            for kc in range(NKC):
                fn = lambda e, kc=kc, tci=tci: e.matmul(
                    out=PS[0][0:1, 0:512], lhsT=onesb[:], rhs=fo(X2B, kc * TL + tci * 512, 512),
                    start=(kc == 0), stop=(kc == NKC - 1))
                if kc == NKC - 1:
                    v_sum2 = P.op("tensor", TM, fn)
                else:
                    P.op("tensor", None, fn)
            for kc in range(NKC):
                P.wait("scalar", TM, v_sum2 if kc == 0 else mmq2[(tci, kc - 1)])
                v_sq2 = P.op("scalar", SA, lambda e, kc=kc, tci=tci: e.activation(
                    out=ba(0, 512), in_=fo(X2B, kc * TL + tci * 512, 512), func=AF.Square))
                P.wait("tensor", SA, v_sq2)
                fn = lambda e, kc=kc: e.matmul(
                    out=PS[1][0:1, 0:512], lhsT=onesb[:], rhs=ba(0, 512),
                    start=(kc == 0), stop=(kc == NKC - 1))
                mmq2[(tci, kc)] = P.op("tensor", TM, fn)
            P.wait("vector", TM, mmq2[(tci, NKC - 1)])
            P.wait("vector", TM, v_sum2)
            P.op("vector", SV, lambda e, tci=tci: e.tensor_scalar(
                out=fo(MRBf, tci * 512, 512, parts=1), in0=PS[0][0:1, 0:512],
                scalar1=1.0 / H, scalar2=None, op0=ALU.mult))
            v_rowread2 = P.op("vector", SV, lambda e, tci=tci: e.tensor_scalar(
                out=fo(RB1f, tci * 512, 512, parts=1), in0=PS[1][0:1, 0:512],
                scalar1=1.0 / H, scalar2=None, op0=ALU.mult))
            P.op("vector", SV, lambda e, tci=tci: e.tensor_mul(
                out=fo(T1f, tci * 512, 512, parts=1), in0=fo(MRBf, tci * 512, 512, parts=1),
                in1=fo(MRBf, tci * 512, 512, parts=1)))
            v_var2 = P.op("vector", SV, lambda e, tci=tci: e.tensor_sub(
                out=fo(RB1f, tci * 512, 512, parts=1), in0=fo(RB1f, tci * 512, 512, parts=1),
                in1=fo(T1f, tci * 512, 512, parts=1)))
            P.wait("scalar", SV, v_var2)
            P.op("scalar", SA, lambda e, tci=tci: e.activation(
                out=fo(RB1f, tci * 512, 512, parts=1), in_=fo(RB1f, tci * 512, 512, parts=1),
                func=AF.Ln, bias=cB[0:1, 0:1]))
            v_rs2 = P.op("scalar", SA, lambda e, tci=tci: e.activation(
                out=fo(RB1f, tci * 512, 512, parts=1), in_=fo(RB1f, tci * 512, 512, parts=1),
                func=AF.Exp, scale=-0.5))
            P.wait("vector", SA, v_rs2)
            v_mr2 = P.op("vector", SV, lambda e, tci=tci: e.tensor_mul(
                out=fo(MRBf, tci * 512, 512, parts=1), in0=fo(MRBf, tci * 512, 512, parts=1),
                in1=fo(RB1f, tci * 512, 512, parts=1)))
        P.wait("sync", SV, v_mr2)
        P.dma("sync", LD, rowb[0:1, 0:TL], fo(RB1f, 0, TL, parts=1))
        v_w = P.dma("sync", LD, rowb[0:1, TL:2 * TL], fo(MRBf, 0, TL, parts=1))
        P.wait("sync", LD, v_w)
        P.dma("sync", LD, fo(RB1f, 0, TL), rowb[0:1, 0:TL].to_broadcast((128, TL)))
        v_bc2 = P.dma("sync", LD, fo(MRBf, 0, TL), rowb[0:1, TL:2 * TL].to_broadcast((128, TL)))
        P.wait("vector", LD, v_bc2)
        outd = {}
        for kc in range(NKC):
            P.op("vector", SV, lambda e, kc=kc: e.tensor_mul(
                out=fo(T1f, 0, TL), in0=fo(X2B, kc * TL, TL), in1=fo(RB1f, 0, TL)))
            tgt = OUTS[0] if kc % 2 == 0 else OUTS[0] + 512
            tgt = ZYT[kc % 2]
            if kc >= 2:
                P.wait("vector", LD, outd[kc - 2])
            v_ap = P.op("vector", SV, lambda e, kc=kc, tgt=tgt: e.tensor_sub(
                out=fa(tgt, TL), in0=fo(T1f, 0, TL), in1=fo(MRBf, 0, TL)))
            P.wait("sync", SV, v_ap)
            outd[kc] = P.dma("sync", LD, out_fm[kc * 128:(kc + 1) * 128, :], fa(tgt, TL))

        P.finish()

        @block.sync
        def _(e):
            for f in P.q["sync"]:
                f(e)

        @block.gpsimd
        def _(e):
            for f in P.q["gpsimd"]:
                f(e)

        @block.scalar
        def _(e):
            for f in P.q["scalar"]:
                f(e)

        @block.vector
        def _(e):
            for f in P.q["vector"]:
                f(e)

        @block.tensor
        def _(e):
            for f in P.q["tensor"]:
                f(e)

    return nc


_NC_CACHE = None


def kernel(L0_attn_x, L0_attn_norm_w, L0_attn_in_proj_w, L0_attn_conv_w, L0_attn_conv_b,
           L0_attn_x_proj_w, L0_attn_dt_proj_w, L0_attn_dt_proj_b, L0_attn_A_log, L0_attn_D,
           L0_attn_out_proj_w, L0_ffn_gate_W, L0_ffn_W1, L0_ffn_b1, L0_ffn_W2, L0_ffn_b2):
    global _NC_CACHE
    f32 = np.float32
    x = np.asarray(L0_attn_x, f32)
    in_proj_w = np.asarray(L0_attn_in_proj_w, f32)
    conv_w = np.asarray(L0_attn_conv_w, f32)
    out_proj_w = np.asarray(L0_attn_out_proj_w, f32)
    x_proj_w = np.asarray(L0_attn_x_proj_w, f32)
    dt_proj_w = np.asarray(L0_attn_dt_proj_w, f32)
    dt_proj_b = np.asarray(L0_attn_dt_proj_b, f32)
    A_log = np.asarray(L0_attn_A_log, f32)
    D = np.asarray(L0_attn_D, f32)
    norm_w = np.asarray(L0_attn_norm_w, f32)
    conv_b = np.asarray(L0_attn_conv_b, f32)
    gate_W = np.asarray(L0_ffn_gate_W, f32)
    W1 = np.asarray(L0_ffn_W1, f32)
    b1 = np.asarray(L0_ffn_b1, f32)
    W2 = np.asarray(L0_ffn_W2, f32)
    b2 = np.asarray(L0_ffn_b2, f32)

    shared = dict(
        wi=np.ascontiguousarray(in_proj_w.reshape(32, 128, NKC, 128).transpose(0, 3, 2, 1)
                                .reshape(32 * 128, NKC * 128)),
        wc=np.ascontiguousarray(conv_w.reshape(4, NKC, 128, 16, 128).transpose(3, 2, 0, 1, 4)
                                .reshape(16 * 128, 4 * NKC * 128)),
        wo=np.ascontiguousarray(out_proj_w.reshape(16, 128, NKC, 128).transpose(0, 3, 2, 1)
                                .reshape(16 * 128, NKC * 128)),
        xpv=np.ascontiguousarray(x_proj_w.reshape(96, NKC, 128).transpose(2, 1, 0)
                                 .reshape(128, NKC * 96)),
        nw=np.ascontiguousarray(norm_w.reshape(NKC, 128).T),
        cbv=np.ascontiguousarray(conv_b.reshape(NKC, 128).T),
        gwv=np.ascontiguousarray(gate_W.reshape(6, NKC, 128).transpose(2, 1, 0)
                                 .reshape(128, NKC * 6)),
        w1f=np.ascontiguousarray(W1.reshape(6 * H, DFF)),
        w2f=np.ascontiguousarray(W2.reshape(6 * DFF, H)),
        b1fm=np.ascontiguousarray(b1.reshape(6, NFC, 128).transpose(0, 2, 1).reshape(6 * 128, NFC)),
        b2fm=np.ascontiguousarray(b2.reshape(6, NKC, 128).transpose(0, 2, 1).reshape(6 * 128, NKC)),
        identi=np.eye(128, dtype=f32),
        onesf=np.ones((128, 1), f32),
        iota8=np.arange(8, dtype=f32).reshape(1, 8),
        cvals=np.array([[EPS, 0.0]], f32),
    )

    in_maps = []
    for c in range(8):
        b, q = divmod(c, 4)
        t0 = q * TL
        xs = np.zeros((H, TC), f32)
        lo = max(0, t0 - HALO)
        xs[:, HALO - (t0 - lo):] = x[b, lo:t0 + TL, :].T
        m = dict(shared)
        m["x_fm"] = np.ascontiguousarray(xs)
        m["dtw"] = np.ascontiguousarray(dt_proj_w[q * 512:(q + 1) * 512, :].T)
        m["dtb"] = np.ascontiguousarray(dt_proj_b[q * 512:(q + 1) * 512].reshape(4, 128).T)
        m["alog"] = np.ascontiguousarray(A_log[q * 512:(q + 1) * 512].reshape(4, 128, NST)
                                         .transpose(1, 0, 2).reshape(128, 4 * NST))
        m["dvec"] = np.ascontiguousarray(D[q * 512:(q + 1) * 512].reshape(4, 128).T)
        m["qoff"] = np.array([[q * 512, q]], np.int32)
        in_maps.append(m)

    if _NC_CACHE is None:
        _NC_CACHE = build_nc()
    res = run_bass_kernel_spmd(_NC_CACHE, in_maps, core_ids=list(range(8)))

    out = np.empty((2, L, H), f32)
    for c in range(8):
        b, q = divmod(c, 4)
        out[b, q * TL:(q + 1) * TL, :] = res.results[c]["out_fm"].T
    return out


# version 10
# speedup vs baseline: 1.0719x; 1.0719x over previous
import contextlib
import numpy as np

import concourse.bass as bass
import concourse.mybir as mybir
from concourse.bass_utils import run_bass_kernel_spmd

F32 = mybir.dt.float32
F32R = mybir.dt.float32r
BF16 = mybir.dt.bfloat16
I32 = mybir.dt.int32
AF = mybir.ActivationFunctionType
ALU = mybir.AluOpType

H = 2048
L = 4096
TL = 1024
HALO = 4
TC = TL + HALO
NKC = 16
DTR = 64
NST = 16
DFF = 1408
NFC = 11
EPS = 1e-5

ENGS = ("sync", "gpsimd", "scalar", "vector", "tensor")
GROUPS = [[0, 1, 2, 3], [4, 5, 6, 7]]
FASZ = 41984


class Sem:
    def __init__(self, h):
        self.h = h
        self.v = 0

    def bump(self, k):
        self.v += k
        return self.v


QDEPTH = 2


class Plan:

    def __init__(self, dnsems, rotsems, esems, ccsem):
        self.ccs = ccsem
        self.q = {e: [] for e in ENGS}
        self._lastwait = {}
        self.dn = dnsems
        self.rots = rotsems
        self.esem = esems
        self.pending = {"sync": [], "gpsimd": []}
        self.echoed = {"sync": 0, "gpsimd": 0}
        self.issued = {"sync": 0, "gpsimd": 0}
        self.gwait = {"sync": [], "gpsimd": []}

    def _emit_echo(self, qn):
        h, v = self.pending[qn].pop(0)
        self.q[qn].append(lambda e, h=h, v=v: e.wait_ge(h, v))
        dn = self.dn[qn]
        self.q[qn].append(lambda e, s=dn.h: e.sem_inc(s, 1))
        dn.bump(1)
        self.echoed[qn] += 1

    def _flush_to(self, qn, val):
        while self.echoed[qn] < val and self.pending[qn]:
            self._emit_echo(qn)

    def _drain_gwaits(self, qn):
        for fn in self.gwait[qn]:
            self.q[qn].append(fn)
        self.gwait[qn] = []

    def _pre_dma(self, qn):
        if len(self.pending[qn]) >= QDEPTH:
            self._emit_echo(qn)
        self._drain_gwaits(qn)

    def dma(self, eng, sem, out, in_, **kw):
        qn = eng
        self._pre_dma(qn)
        rot = self.rots[qn][self.issued[qn] % 2]
        self.issued[qn] += 1
        v = rot.bump(16)
        self.q[qn].append(lambda e, out=out, in_=in_, kw=dict(kw), s=rot.h:
                          e.dma_start(out=out, in_=in_, **kw).then_inc(s, 16))
        self.pending[qn].append((rot.h, v))
        return self.issued[qn]

    def idma(self, eng, sem, out, in_, idx, element_offset=0):
        qn = eng
        self._pre_dma(qn)
        rot = self.rots[qn][self.issued[qn] % 2]
        self.issued[qn] += 1
        v = rot.bump(16)
        self.q[qn].append(lambda e, out=out, in_=in_, idx=idx, eo=element_offset, s=rot.h:
                          e.indirect_dma_start(out=out, out_offset=None, in_=in_,
                                               in_offset=bass.IndirectOffsetOnAxis(ap=idx, axis=0),
                                               element_offset=eo).then_inc(s, 16))
        self.pending[qn].append((rot.h, v))
        return self.issued[qn]

    def cc(self, eng, sem, kind, in_ap, out_ap):
        qn = eng
        self._pre_dma(qn)
        rot = self.ccs
        self.issued[qn] += 1
        v = rot.bump(1)
        self.q[qn].append(lambda e, k=kind, i=in_ap, o=out_ap, s=rot.h:
                          e.collective_compute(k, ALU.bypass, replica_groups=GROUPS,
                                               ins=[i.opt()], outs=[o.opt()]).then_inc(s, 1))
        self.pending[qn].append((rot.h, v))
        return self.issued[qn]

    def op(self, eng, sem, fn):
        if eng == "gpsimd":
            self._drain_gwaits("gpsimd")
        if eng in ("vector", "scalar", "gpsimd"):
            es = self.esem[eng]
            if es.v > 0:
                self.q[eng].append(lambda e, s=es.h, v=es.v: e.wait_ge(s, v))
            v = es.bump(1)
            self.q[eng].append(lambda e, fn=fn, s=es.h: fn(e).then_inc(s, 1))
            return v
        if sem is None:
            self.q[eng].append(fn)
            return None
        v = sem.bump(1)
        self.q[eng].append(lambda e, fn=fn, s=sem.h: fn(e).then_inc(s, 1))
        return v

    def wait(self, eng, sem, val):
        if val is None or val <= 0:
            return
        for qn in ("sync", "gpsimd"):
            if sem is self.dn[qn]:
                self._flush_to(qn, val)
                break
        key = (eng, id(sem))
        if self._lastwait.get(key, -1) >= val:
            return
        self._lastwait[key] = val
        fn = lambda e, s=sem.h, v=val: e.wait_ge(s, v)
        if eng in ("sync", "gpsimd"):
            self.gwait[eng].append(fn)
        else:
            self.q[eng].append(fn)

    def finish(self):
        for qn in ("sync", "gpsimd"):
            while self.pending[qn]:
                self._emit_echo(qn)
            self._drain_gwaits(qn)


def build_nc():
    nc = bass.Bass("TRN2", target_bir_lowering=False)

    def din(name, shape, dt):
        return nc.dram_tensor(name, shape, dt, kind="ExternalInput").ap()

    def dint(name, shape, dt):
        return nc.dram_tensor(name, shape, dt).ap()

    x_fm = din("x_fm", [H, TC], F32R)
    wi = din("wi", [32 * 128, NKC * 128], F32R)
    wc = din("wc", [16 * 128, 4 * NKC * 128], F32R)
    wo = din("wo", [16 * 128, NKC * 128], F32R)
    xpv = din("xpv", [128, NKC * 96], F32R)
    dtw = din("dtw", [DTR, 512], F32)
    dtb = din("dtb", [128, 4], F32)
    alog = din("alog", [128, 4 * NST], F32)
    dvec = din("dvec", [128, 4], F32)
    nw = din("nw", [128, NKC], F32)
    cbv = din("cbv", [128, NKC], F32)
    gwv = din("gwv", [128, NKC * 6], F32)
    w1f = din("w1f", [6 * H, DFF], F32)
    w2f = din("w2f", [6 * DFF, H], F32)
    b1fm = din("b1fm", [6 * 128, NFC], F32)
    b2fm = din("b2fm", [6 * 128, NKC], F32)
    identi = din("identi", [128, 128], F32)
    onesf = din("onesf", [128, 1], F32)
    iota8 = din("iota8", [1, 8], F32)
    qoff = din("qoff", [1, 2], I32)
    cvals = din("cvals", [1, 2], F32)

    out_fm = nc.dram_tensor("out_fm", [H, TL], F32, kind="ExternalOutput").ap()

    zdram = dint("zdram", [H, TL], F32)
    xcst = dint("xcst", [H, TL], F32)
    dbcst = dint("dbcst", [96, TL], F32)
    xcg = dint("xcg", [4 * H, TL], F32)
    dbcg = dint("dbcg", [4 * 96, TL], F32)
    bmcmd = dint("bmcmd", [32, L], F32R)
    yst = dint("yst", [512, L], F32)
    yg = dint("yg", [H, L], F32)
    ygv = yg.rearrange("c (r t) -> (c r) t", t=TL)
    gst = dint("gst", [128, NKC], F32)
    gg = dint("gg", [512, NKC], F32)
    rowb = dint("rowb", [1, 2 * TC], F32)
    wrow = dint("wrow", [1, 8], F32)

    with contextlib.ExitStack() as ctx:
        sbt = lambda n, s, t: ctx.enter_context(nc.sbuf_tensor(n, s, t))
        ARENA = sbt("ARENA", [128, FASZ * 4], mybir.dt.uint8)
        _abase = nc.lookup_mloc(ARENA).addr

        def alias(name, shape, dtype, byte_off):
            h = nc.virtual_sbuf_tensor(name, shape, dtype)
            nc._tensor(h.name, h.shape, h.dtype, type="SB")
            m = nc.lookup_mloc(h)
            m.allocated = True
            m.addr = _abase + byte_off
            return h

        FRt = alias("FRt", [128, 32896], F32R, 0)
        SQRt = alias("SQRt", [128, 1028], F32R, 32896 * 4)
        RBRt = alias("RBRt", [128, 1028], F32, 33928 * 4)
        WINRt = alias("WINRt", [128, 4976], F32R, 34960 * 4)
        ZTRt = alias("ZTRt", [128, 2048], F32, 39936 * 4)
        ABf = alias("ABf", [128, 1024], F32, 16384 * 4)
        BTf = alias("BTf", [128, 1024], F32, 17408 * 4)
        STf = alias("STf", [128, 1024], F32, 18432 * 4)
        RB1f = alias("RB1f", [128, 1024], F32, 32768 * 4)
        MRBf = alias("MRBf", [128, 1024], F32, 33792 * 4)
        T1f = alias("T1f", [128, 1024], F32, 34816 * 4)
        XPWr = alias("XPWr", [128, 1536], F32R, 39936 * 4)
        GB = alias("GB", [128, 11264], BF16, 0)
        X2B = alias("X2B", [128, 16384], BF16, 16384 * 2)
        W1B = alias("W1B", [128, 22528], BF16, 34816 * 2)

        BA = sbt("BA", [128, 16384], BF16)
        SM = sbt("SM", [128, 1088], F32)
        SR = sbt("SR", [128, 768], F32R)
        SI = sbt("SI", [128, 64], I32)
        R8 = sbt("R8", [1, 64], F32)
        onesb = sbt("onesb", [128, 1], BF16)
        dtbs = SM[:, 0:4]
        alogs = SM[:, 4:68]
        aneg = SM[:, 68:132]
        dvs = SM[:, 132:136]
        nws = SM[:, 136:152]
        cbs = SM[:, 152:168]
        gsum = SM[:, 168:184]
        ggr = SM[:, 184:248]
        m128 = SM[:, 248:264]
        smrB = SM[:, 264:265]
        wB = SM[:, 265:273]
        ecol = SM[:, 273:279]
        b1g = SM[:, 279:301]
        b2g = SM[:, 301:333]
        b2c = SM[:, 333:349]
        carry = SM[:, 349:413]
        qcolf = SM[:, 413:415]
        cB = SM[:, 415:417]
        iotaf = SM[:, 417:418]
        iota4f = SM[:, 418:419]
        idxsf = SM[:, 419:420]
        dxs = SM[:, 448:960]
        dtws = SR[0:64, 0:512]
        gws = SR[:, 512:608]
        idents = SR[:, 608:736]
        onest = SR[:, 736:737]
        m128r = SR[:, 737:753]
        qcoli = SI[:, 0:2]
        iotai = SI[:, 2:3]
        iota4i = SI[:, 3:4]
        idxg = SI[:, 4:20]
        idxy = SI[:, 20:36]
        idxm = SI[:, 36:52]
        idxtb = SI[:, 52:53]
        iota8s = R8[:, 0:8]
        ls8 = R8[:, 8:16]
        mx8 = R8[:, 16:24]
        eq8 = R8[:, 24:32]
        grow = R8[:, 32:40]
        gtmp = R8[:, 40:48]

        PS = [ctx.enter_context(nc.psum_tensor(f"ps{i}", [128, 512], F32)) for i in range(8)]

        LD = Sem(ctx.enter_context(nc.semaphore("LD")))
        GP = Sem(ctx.enter_context(nc.semaphore("GP")))
        SA = Sem(ctx.enter_context(nc.semaphore("SA")))
        SV = Sem(ctx.enter_context(nc.semaphore("SV")))
        TM = Sem(ctx.enter_context(nc.semaphore("TM")))
        GC = Sem(ctx.enter_context(nc.semaphore("GC")))
        LDa = Sem(ctx.enter_context(nc.semaphore("LDa")))
        LDb = Sem(ctx.enter_context(nc.semaphore("LDb")))
        GPa = Sem(ctx.enter_context(nc.semaphore("GPa")))
        GPb = Sem(ctx.enter_context(nc.semaphore("GPb")))
        CCS = Sem(ctx.enter_context(nc.semaphore("CCS")))

        block = ctx.enter_context(nc.Block())
        P = Plan({"sync": LD, "gpsimd": GP}, {"sync": [LDa, LDb], "gpsimd": [GPa, GPb]},
                 {"vector": SV, "scalar": SA, "gpsimd": GC, "tensor": TM}, CCS)

        REGIONS = ((0, 32896, FRt), (32896, 33924, SQRt), (33928, 34956, RBRt),
                   (34960, 39936, WINRt), (39936, 41984, ZTRt))

        def fa(off, ln, parts=128, p0=0):
            for s, e2, t in REGIONS:
                if s <= off and off + ln <= e2:
                    return t[p0:p0 + parts, off - s:off - s + ln]
            raise AssertionError((off, ln))

        fr = fa

        def faf(off, ln, parts=128, p0=0):
            return fa(off, ln, parts, p0).bitcast(F32)

        def fo(t, off, ln, parts=128, p0=0):
            return t[p0:p0 + parts, off:off + ln]

        def ba(off, ln, parts=128, p0=0):
            assert 0 <= off and off + ln <= 16384, (off, ln)
            return BA[p0:p0 + parts, off:off + ln]

        vx = [P.dma("sync", LD, fa(kc * TC, TC), x_fm[kc * 128:(kc + 1) * 128, :])
              for kc in range(NKC)]
        for dst, src in ((dtbs, dtb), (alogs, alog), (dvs, dvec), (nws, nw), (cbs, cbv),
                         (iota8s, iota8)):
            P.dma("sync", LD, dst[:], src)
        P.dma("sync", LD, qcoli, qoff.to_broadcast((128, 2)))
        v_consts = P.dma("sync", LD, cB, cvals.to_broadcast((128, 2)))
        P.dma("gpsimd", GP, dtws, dtw)
        P.dma("gpsimd", GP, gws, gwv)
        P.dma("gpsimd", GP, idents, identi)
        v_gp0 = P.dma("gpsimd", GP, onest, onesf)
        P.op("gpsimd", GC, lambda e: e.iota(iotai, pattern=[[0, 1]], base=0, channel_multiplier=1))
        v_iota = P.op("gpsimd", GC, lambda e: e.iota(iota4i, pattern=[[0, 1]], base=0, channel_multiplier=4))

        P.wait("vector", LD, v_consts)
        P.wait("vector", GC, v_iota)
        P.op("vector", SV, lambda e: e.memset(carry, 0.0))
        P.op("vector", SV, lambda e: e.memset(grow, 0.0))
        P.op("vector", SV, lambda e: e.memset(onesb[:], 1.0))
        P.op("vector", SV, lambda e: e.tensor_copy(out=iotaf, in_=iotai))
        P.op("vector", SV, lambda e: e.tensor_copy(out=iota4f, in_=iota4i))
        P.op("vector", SV, lambda e: e.tensor_copy(out=qcolf, in_=qcoli))
        P.wait("scalar", LD, v_consts)
        v_expal = P.op("scalar", SA, lambda e: e.activation(out=aneg, in_=alogs, func=AF.Exp))
        P.wait("vector", SA, v_expal)
        v_aneg = P.op("vector", SV, lambda e: e.tensor_scalar(
            out=aneg, in0=aneg, scalar1=-1.0, scalar2=None, op0=ALU.mult))

        XHN = 0
        XZ0 = 16448
        SQ = (32896, 32896)
        RB = 33928
        WIN = (34960, 37008)
        ZT = (39936, 40960)
        chunks = ((0, HALO), (HALO, HALO + 512), (HALO + 512, TC))

        P.wait("tensor", GP, v_gp0)
        var_mm_v = [None] * NKC
        for kc in range(NKC):
            P.wait("scalar", LD, vx[kc])
            if kc >= 1:
                P.wait("scalar", TM, var_mm_v[kc - 1])
            v_sq = P.op("scalar", SA, lambda e, kc=kc: e.activation(
                out=fr(SQ[kc % 2], TC), in_=fa(XHN + kc * TC, TC), func=AF.Square))
            P.wait("tensor", SA, v_sq)
            for ci, (c0, c1) in enumerate(chunks):
                fn = lambda e, kc=kc, ci=ci, c0=c0, c1=c1: e.matmul(
                    out=PS[5 + ci][0:1, 0:(c1 - c0)], lhsT=onest,
                    rhs=fr(SQ[kc % 2] + c0, c1 - c0),
                    start=(kc == 0), stop=(kc == NKC - 1))
                if ci == 2:
                    var_mm_v[kc] = P.op("tensor", TM, fn)
                else:
                    P.op("tensor", None, fn)
        P.wait("scalar", TM, var_mm_v[NKC - 1])
        for ci, (c0, c1) in enumerate(chunks):
            P.op("scalar", SA, lambda e, ci=ci, c0=c0, c1=c1: e.activation(
                out=fa(RB + c0, c1 - c0, parts=1), in_=PS[5 + ci][0:1, 0:(c1 - c0)],
                func=AF.Ln, scale=1.0 / H, bias=cB[0:1, 0:1]))
            v_rr = P.op("scalar", SA, lambda e, c0=c0, c1=c1: e.activation(
                out=fa(RB + c0, c1 - c0, parts=1), in_=fa(RB + c0, c1 - c0, parts=1),
                func=AF.Exp, scale=-0.5))
        P.wait("sync", SA, v_rr)
        v_w = P.dma("sync", LD, rowb[0:1, 0:TC], fa(RB, TC, parts=1))
        P.wait("sync", LD, v_w)
        v_rstdB = P.dma("sync", LD, fa(RB, TC), rowb[0:1, 0:TC].to_broadcast((128, TC)))

        P.wait("vector", LD, v_rstdB)
        hn_v = [None] * NKC
        for kc in range(NKC):
            hn_v[kc] = P.op("vector", SV, lambda e, kc=kc: e.scalar_tensor_tensor(
                out=fr(XHN + kc * TC, TC), in0=fa(XHN + kc * TC, TC), scalar=nws[:, kc:kc + 1],
                in1=fa(RB, TC), op0=ALU.mult, op1=ALU.mult))
        v_hn_all = hn_v[-1]

        ip_load_v = [None] * 32
        ip_mm_v = [None] * 32
        ip_evac_v = [None] * 32
        z_dma_v = {}

        def ip_compute(oc):
            bk = 0 if oc % 2 == 0 else 3
            P.wait("tensor", GP, ip_load_v[oc])
            P.wait("tensor", SV, v_hn_all)
            if oc >= 2:
                P.wait("tensor", SA, ip_evac_v[oc - 2])
            wslot = WIN[oc % 2]
            cl = chunks if oc < 16 else chunks[1:]
            for ci, (c0, c1) in enumerate(cl):
                for kc in range(NKC):
                    fn = lambda e, kc=kc, bk=bk, ci=ci, c0=c0, c1=c1, wslot=wslot: e.matmul(
                        out=PS[bk + ci][:, 0:(c1 - c0)],
                        lhsT=fr(wslot + kc * 128, 128),
                        rhs=fr(XHN + kc * TC + c0, c1 - c0),
                        start=(kc == 0), stop=(kc == NKC - 1))
                    if ci == len(cl) - 1 and kc == NKC - 1:
                        ip_mm_v[oc] = P.op("tensor", TM, fn)
                    else:
                        P.op("tensor", None, fn)
            P.wait("scalar", TM, ip_mm_v[oc])
            if oc < 16:
                for ci, (c0, c1) in enumerate(cl):
                    ip_evac_v[oc] = P.op("scalar", SA, lambda e, oc=oc, bk=bk, ci=ci, c0=c0, c1=c1:
                                         e.activation(out=fr(XZ0 + oc * TC + c0, c1 - c0),
                                                      in_=PS[bk + ci][:, 0:(c1 - c0)], func=AF.Copy))
            else:
                zslot = ZT[oc % 2]
                if oc >= 18:
                    P.wait("scalar", LD, z_dma_v[oc - 2])
                for ci, (c0, c1) in enumerate(cl):
                    ip_evac_v[oc] = P.op("scalar", SA, lambda e, zslot=zslot, bk=bk, ci=ci, c0=c0, c1=c1:
                                         e.activation(out=fa(zslot + (c0 - HALO), c1 - c0),
                                                      in_=PS[bk + ci][:, 0:(c1 - c0)], func=AF.Copy))
                P.wait("sync", SA, ip_evac_v[oc])
                z_dma_v[oc] = P.dma("sync", LD, zdram[(oc - 16) * 128:(oc - 15) * 128, :],
                                    fa(zslot, TL))

        for i in range(34):
            if i >= 2:
                ip_compute(i - 2)
            if i < 32:
                if i >= 2:
                    P.wait("gpsimd", TM, ip_mm_v[i - 2])
                ip_load_v[i] = P.dma("gpsimd", GP, fr(WIN[i % 2], 2048), wi[i * 128:(i + 1) * 128, :])

        XCB = 0
        cv_steps = [(oc, k) for oc in range(16) for k in range(4)]
        cvk_mm = {}
        cv_evac_v = [None] * 16
        xc_dma_v = [None] * 16
        cv_load_v = {}
        cchunks = ((HALO, HALO + 512), (HALO + 512, TC))

        def cv_step_compute(j):
            oc, k = cv_steps[j]
            bk = 0 if oc % 2 == 0 else 2
            P.wait("tensor", GP, cv_load_v[(oc, k)])
            if k == 0:
                P.wait("tensor", SA, ip_evac_v[14])
                P.wait("tensor", SA, ip_evac_v[15])
                if oc >= 2:
                    P.wait("tensor", SA, cv_evac_v[oc - 2])
            wslot = WIN[j % 2]
            for ci, (c0, c1) in enumerate(cchunks):
                for kc in range(NKC):
                    fn = lambda e, k=k, kc=kc, bk=bk, ci=ci, c0=c0, wslot=wslot: e.matmul(
                        out=PS[bk + ci][:, 0:512],
                        lhsT=fr(wslot + kc * 128, 128),
                        rhs=fr(XZ0 + kc * TC + (c0 - HALO) + k + 1, 512),
                        start=(k == 0 and kc == 0), stop=(k == 3 and kc == NKC - 1))
                    if ci == 1 and kc == NKC - 1:
                        cvk_mm[(oc, k)] = P.op("tensor", TM, fn)
                    else:
                        P.op("tensor", None, fn)
            if k == 3:
                P.wait("scalar", TM, cvk_mm[(oc, 3)])
                P.wait("scalar", TM, ip_mm_v[31])
                for ci in range(2):
                    cv_evac_v[oc] = P.op("scalar", SA, lambda e, oc=oc, bk=bk, ci=ci: e.activation(
                        out=fr(XCB + oc * TL + ci * 512, 512), in_=PS[bk + ci][:, 0:512],
                        func=AF.Silu, bias=cbs[:, oc:oc + 1]))
                P.wait("sync", SA, cv_evac_v[oc])
                xc_dma_v[oc] = P.dma("sync", LD, xcst[oc * 128:(oc + 1) * 128, :],
                                     faf(XCB + oc * TL, TL))

        for j in range(len(cv_steps) + 2):
            if j >= 2:
                cv_step_compute(j - 2)
            if j < len(cv_steps):
                oc, k = cv_steps[j]
                if j < 2:
                    P.wait("gpsimd", TM, ip_mm_v[31])
                    P.wait("gpsimd", LD, z_dma_v[31])
                else:
                    P.wait("gpsimd", TM, cvk_mm[cv_steps[j - 2]])
                cv_load_v[(oc, k)] = P.dma("gpsimd", GP, fr(WIN[j % 2], 2048),
                                           wc[oc * 128:(oc + 1) * 128, k * 2048:(k + 1) * 2048])

        xpw = XPWr[:, 0:1536]
        P.wait("gpsimd", LD, z_dma_v[31])
        v_xpw = P.dma("gpsimd", GP, xpw, xpv)
        DBC = SQ[0]
        xp_mm_v = [None] * 2
        v_dbc = None
        P.wait("tensor", SA, cv_evac_v[15])
        P.wait("tensor", SA, ip_evac_v[31])
        P.wait("tensor", GP, v_xpw)
        for tci in range(2):
            for kc in range(NKC):
                fn = lambda e, kc=kc, tci=tci: e.matmul(
                    out=PS[4 + tci][0:96, 0:512],
                    lhsT=xpw[:, kc * 96:(kc + 1) * 96],
                    rhs=fr(XCB + kc * TL + tci * 512, 512),
                    start=(kc == 0), stop=(kc == NKC - 1))
                if kc == NKC - 1:
                    xp_mm_v[tci] = P.op("tensor", TM, fn)
                else:
                    P.op("tensor", None, fn)
            P.wait("vector", TM, xp_mm_v[tci])
            v_dbc = P.op("vector", SV, lambda e, tci=tci: e.tensor_copy(
                out=fa(DBC + tci * 512, 512, parts=96), in_=PS[4 + tci][0:96, 0:512]))
        P.wait("sync", SV, v_dbc)
        v_dbcst = P.dma("sync", LD, dbcst, faf(DBC, TL, parts=96))

        P.wait("gpsimd", LD, max(xc_dma_v[15], v_dbcst))
        v_cc_xc = P.cc("gpsimd", GP, "AllGather", xcst, xcg)
        v_cc_dbc = P.cc("gpsimd", GP, "AllGather", dbcst, dbcg)

        DEL2 = 0
        U2 = 4096
        XCC = 8192
        DPRE = 12288
        AB = 16384
        BT = 17408
        ST2 = 18432
        SCT = 19456
        BMB = 23552
        CMB = 24576
        YSC = 25600
        SPS = 26624
        BMS = 27648

        P.wait("gpsimd", GP, v_cc_dbc)
        v_dpre = None
        v_bms = None
        for r in range(4):
            v_dpre = P.dma("gpsimd", GP, fr(DPRE + r * TL, TL, parts=64),
                           dbcg[r * 96:r * 96 + 64, :])
            v_bms = P.dma("gpsimd", GP, fr(BMS + r * TL, TL, parts=32),
                          dbcg[r * 96 + 64:r * 96 + 96, :])
        P.wait("sync", GP, v_bms)
        v_bmcmd = P.dma("sync", LD, bmcmd, fa(BMS, L, parts=32))

        v_idx = None
        for r in range(4):
            for hb in range(4):
                j = r * 4 + hb
                P.op("vector", SV, lambda e, r=r, hb=hb: e.tensor_scalar(
                    out=idxsf, in0=iotaf, scalar1=qcolf[:, 0:1],
                    scalar2=float(r * 2048 + hb * 128), op0=ALU.add, op1=ALU.add))
                v_idx = P.op("vector", SV, lambda e, j=j: e.tensor_copy(
                    out=idxg[:, j:j + 1], in_=idxsf))
        v_idxy = None
        for kc in range(NKC):
            P.op("vector", SV, lambda e, kc=kc: e.tensor_scalar(
                out=idxsf, in0=iota4f, scalar1=qcolf[:, 1:2],
                scalar2=float(kc * 512), op0=ALU.add, op1=ALU.add))
            v_idxy = P.op("vector", SV, lambda e, kc=kc: e.tensor_copy(
                out=idxy[:, kc:kc + 1], in_=idxsf))
        P.wait("gpsimd", SV, v_idx)
        P.wait("gpsimd", GP, v_cc_xc)

        NIT = 128
        bm_ld_v = [None] * NIT
        xcc_v = {}
        u_v = {}
        a_v = {}
        tts_v = {}
        sc_v = {}
        mm_v = {}
        yev_v = {}
        ysd_v = {}
        dt_relu = {}

        def emit_tc_prep(tc):
            par = tc % 2
            r = tc // 2
            ybk = (tc % 2) * 4
            for hb in range(4):
                if tc >= 2:
                    P.wait("gpsimd", SV, yev_v[(tc - 2, hb)])
                xcc_v[(tc, hb)] = P.idma(
                    "gpsimd", GP, fr(XCC + par * 2048 + hb * 512, 512), xcg,
                    idxg[:, (r * 4 + hb):(r * 4 + hb) + 1],
                    element_offset=(tc % 2) * 512)
                bk = ybk + hb
                P.wait("tensor", GP, v_dpre)
                if tc >= 2:
                    P.wait("tensor", SV, yev_v[(tc - 2, hb)])
                v_dtmm = P.op("tensor", TM, lambda e, hb=hb, tc=tc, bk=bk: e.matmul(
                    out=PS[bk][:, 0:512], lhsT=dtws[:, hb * 128:(hb + 1) * 128],
                    rhs=fr(DPRE + tc * 512, 512, parts=64), start=True, stop=True))
                m = tc * 4 + hb
                P.wait("scalar", TM, v_dtmm)
                if m >= 2:
                    ptc, phb = divmod(m - 2, 4)
                    P.wait("scalar", SV, u_v[(ptc, phb)])
                sps = SPS + (m % 2) * 512
                dslot = DEL2 + par * 2048 + hb * 512
                P.op("scalar", SA, lambda e, bk=bk, sps=sps, hb=hb: e.activation(
                    out=fa(sps, 512), in_=PS[bk][:, 0:512], func=AF.Abs, bias=dtbs[:, hb:hb + 1]))
                P.op("scalar", SA, lambda e, sps=sps: e.activation(
                    out=fa(sps, 512), in_=fa(sps, 512), func=AF.Exp, scale=-1.0))
                P.op("scalar", SA, lambda e, sps=sps: e.activation(
                    out=fa(sps, 512), in_=fa(sps, 512), func=AF.Ln, bias=1.0))
                v_relu = P.op("scalar", SA, lambda e, bk=bk, hb=hb, dslot=dslot: e.activation(
                    out=fa(dslot, 512), in_=PS[bk][:, 0:512], func=AF.Relu, bias=dtbs[:, hb:hb + 1]))
                dt_relu[(tc, hb)] = v_relu
                if tc >= 2:
                    P.wait("vector", SA, a_v[(tc - 2, 15, hb)])
                P.wait("vector", SA, v_relu)
                P.op("vector", SV, lambda e, dslot=dslot, sps=sps: e.tensor_add(
                    out=fa(dslot, 512), in0=fa(dslot, 512), in1=fa(sps, 512)))
                P.wait("vector", GP, xcc_v[(tc, hb)])
                u_v[(tc, hb)] = P.op("vector", SV, lambda e, dslot=dslot, par=par, hb=hb: e.tensor_mul(
                    out=fa(U2 + par * 2048 + hb * 512, 512), in0=fa(dslot, 512),
                    in1=fa(XCC + par * 2048 + hb * 512, 512)))

        def emit_bm_load(i):
            tc, n = divmod(i, 16)
            par = i % 2
            if i >= 2:
                P.wait("sync", SV, sc_v[(i - 2, 3)])
            P.wait("sync", LD, v_bmcmd)
            P.dma("sync", LD, fa(BMB + par * 512, 512),
                  bmcmd[n:n + 1, tc * 512:(tc + 1) * 512].to_broadcast((128, 512)))
            bm_ld_v[i] = P.dma("sync", LD, fa(CMB + par * 512, 512),
                               bmcmd[16 + n:17 + n, tc * 512:(tc + 1) * 512].to_broadcast((128, 512)))

        def emit_scan_iter(i):
            tc, n = divmod(i, 16)
            par = i % 2
            tpar = tc % 2
            ybk = (tc % 2) * 4
            for hb in range(4):
                cidx = hb * 16 + n
                g = i * 4 + hb
                aslot = (g % 2) * 512
                dslot = DEL2 + tpar * 2048 + hb * 512
                if g >= 2:
                    pi, phb = divmod(g - 2, 4)
                    P.wait("scalar", SV, tts_v[(pi, phb)])
                P.wait("scalar", SV, u_v[(tc, hb)])
                P.wait("scalar", SV, v_aneg)
                a_v[(tc, n, hb)] = P.op("scalar", SA, lambda e, hb=hb, n=n, aslot=aslot, dslot=dslot:
                                        e.activation(out=fo(ABf, aslot, 512), in_=fa(dslot, 512),
                                                     func=AF.Exp,
                                                     scale=aneg[:, hb * 16 + n:hb * 16 + n + 1]))
                P.wait("vector", LD, bm_ld_v[i])
                P.op("vector", SV, lambda e, hb=hb, tpar=tpar, par=par: e.tensor_mul(
                    out=fo(BTf, par * 512, 512), in0=fa(U2 + tpar * 2048 + hb * 512, 512),
                    in1=fa(BMB + par * 512, 512)))
                P.wait("vector", SA, a_v[(tc, n, hb)])
                tts_v[(i, hb)] = P.op("vector", SV, lambda e, par=par, aslot=aslot, cidx=cidx:
                                      e.tensor_tensor_scan(out=fo(STf, par * 512, 512),
                                                           data0=fo(ABf, aslot, 512),
                                                           data1=fo(BTf, par * 512, 512),
                                                           initial=carry[:, cidx:cidx + 1],
                                                           op0=ALU.mult, op1=ALU.add))
                P.op("vector", SV, lambda e, par=par, cidx=cidx: e.tensor_copy(
                    out=carry[:, cidx:cidx + 1], in_=fo(STf, par * 512 + 511, 1)))
                scs = SCT + (hb * 2 + par) * 512
                if i >= 2:
                    P.wait("vector", TM, mm_v[(i - 2, hb)])
                sc_v[(i, hb)] = P.op("vector", SV, lambda e, par=par, scs=scs: e.tensor_mul(
                    out=fr(scs, 512), in0=fo(STf, par * 512, 512),
                    in1=fa(CMB + par * 512, 512)))
                P.wait("tensor", SV, sc_v[(i, hb)])
                mm_v[(i, hb)] = P.op("tensor", TM, lambda e, scs=scs, ybk=ybk, hb=hb, n=n: e.matmul(
                    out=PS[ybk + hb][:, 0:512], lhsT=idents, rhs=fr(scs, 512),
                    start=(n == 0), stop=(n == 15)))

        def emit_y_evac(tc):
            ybk = (tc % 2) * 4
            tpar = tc % 2
            for hb in range(4):
                P.wait("vector", TM, mm_v[(tc * 16 + 15, hb)])
                j = tc * 4 + hb
                ysl = YSC + (j % 2) * 512
                if j >= 2:
                    P.wait("vector", LD, ysd_v[j - 2])
                P.op("vector", SV, lambda e, hb=hb, tpar=tpar: e.tensor_scalar(
                    out=dxs[:], in0=fa(XCC + tpar * 2048 + hb * 512, 512),
                    scalar1=dvs[:, hb:hb + 1], scalar2=None, op0=ALU.mult))
                yev_v[(tc, hb)] = P.op("vector", SV, lambda e, ybk=ybk, hb=hb, ysl=ysl: e.tensor_add(
                    out=fa(ysl, 512), in0=PS[ybk + hb][:, 0:512], in1=dxs[:]))
                P.wait("sync", SV, yev_v[(tc, hb)])
                ysd_v[j] = P.dma("sync", LD, yst[hb * 128:(hb + 1) * 128, tc * 512:(tc + 1) * 512],
                                 faf(ysl, 512))

        emit_tc_prep(0)
        emit_bm_load(0)
        for i in range(NIT):
            tc, n = divmod(i, 16)
            if n == 0 and tc + 1 < 8:
                emit_tc_prep(tc + 1)
            if i + 1 < NIT:
                emit_bm_load(i + 1)
            emit_scan_iter(i)
            if n == 15:
                emit_y_evac(tc)

        P.wait("gpsimd", LD, ysd_v[31])
        v_cc_y = P.cc("gpsimd", GP, "AllGather", yst, yg)

        XR = 0
        V3 = 16384
        RB1 = 32768
        MRB = 33792
        T1 = 34816
        WOS = (35840, 37888)
        ZYT = (39936, 40960)
        OUTS = (39936, 40960)
        SQ3 = WOS[0]
        G3B = 0
        XR2B = 16384
        W1EB = 34816

        vv_v = [None] * NKC
        sz_done = [None] * NKC
        P.wait("sync", LD, ysd_v[31])
        P.wait("vector", LD, bm_ld_v[NIT - 1])
        P.wait("gpsimd", GP, v_cc_y)
        P.wait("gpsimd", SV, v_idxy)
        for kc in range(NKC):
            zslot, yslot = ZYT
            if kc >= 1:
                P.wait("sync", SA, sz_done[kc - 1])
                P.wait("sync", SV, vv_v[kc - 1])
            v_z = P.dma("sync", LD, fa(zslot, TL), zdram[kc * 128:(kc + 1) * 128, :])
            if kc >= 1:
                P.wait("gpsimd", SV, vv_v[kc - 1])
            v_y = P.idma("gpsimd", GP, fa(yslot, TL), ygv, idxy[:, kc:kc + 1])
            P.wait("scalar", LD, v_z)
            sz_done[kc] = P.op("scalar", SA, lambda e, zslot=zslot: e.activation(
                out=fa(zslot, TL), in_=fa(zslot, TL), func=AF.Silu))
            P.wait("vector", SA, sz_done[kc])
            P.wait("vector", GP, v_y)
            vv_v[kc] = P.op("vector", SV, lambda e, kc=kc, zslot=zslot, yslot=yslot: e.tensor_mul(
                out=fr(V3 + kc * TL, TL), in0=fa(yslot, TL), in1=fa(zslot, TL)))

        xr_v = [P.dma("sync", LD, fa(XR + kc * TL, TL), x_fm[kc * 128:(kc + 1) * 128, HALO:TC])
                for kc in range(NKC)]

        op_load_v = [None] * NKC
        op_mm_v = [None] * NKC
        op_evac_v = [None] * NKC

        def op_compute(oc):
            bk = 0 if oc % 2 == 0 else 2
            P.wait("tensor", GP, op_load_v[oc])
            P.wait("tensor", SV, vv_v[NKC - 1])
            if oc >= 2:
                P.wait("tensor", SV, op_evac_v[oc - 2])
            for tci in range(2):
                for kc in range(NKC):
                    fn = lambda e, oc=oc, tci=tci, kc=kc, bk=bk: e.matmul(
                        out=PS[bk + tci][:, 0:512],
                        lhsT=fr(WOS[oc % 2] + kc * 128, 128),
                        rhs=fr(V3 + kc * TL + tci * 512, 512),
                        start=(kc == 0), stop=(kc == NKC - 1))
                    if tci == 1 and kc == NKC - 1:
                        op_mm_v[oc] = P.op("tensor", TM, fn)
                    else:
                        P.op("tensor", None, fn)
            P.wait("vector", TM, op_mm_v[oc])
            P.wait("vector", LD, xr_v[oc])
            for tci in range(2):
                op_evac_v[oc] = P.op("vector", SV, lambda e, oc=oc, tci=tci, bk=bk:
                                     e.scalar_tensor_tensor(
                                         out=fr(XR + oc * TL + tci * 512, 512),
                                         in0=fa(XR + oc * TL + tci * 512, 512), scalar=2.0,
                                         in1=PS[bk + tci][:, 0:512],
                                         op0=ALU.mult, op1=ALU.add))

        for i in range(18):
            if i >= 2:
                op_compute(i - 2)
            if i < 16:
                if i >= 2:
                    P.wait("gpsimd", TM, op_mm_v[i - 2])
                else:
                    P.wait("gpsimd", SV, sc_v[(NIT - 1, 3)])
                op_load_v[i] = P.dma("gpsimd", GP, fr(WOS[i % 2], 2048), wo[i * 128:(i + 1) * 128, :])

        mmq = {}
        v_mr = None
        v_rowread = None
        for tci in range(2):
            if v_rowread is not None:
                P.wait("tensor", SV, v_rowread)
            for kc in range(NKC):
                P.wait("tensor", SV, op_evac_v[kc])
                fn = lambda e, kc=kc, tci=tci: e.matmul(
                    out=PS[6][0:1, 0:512], lhsT=onest, rhs=fr(XR + kc * TL + tci * 512, 512),
                    start=(kc == 0), stop=(kc == NKC - 1))
                if kc == NKC - 1:
                    v_sum = P.op("tensor", TM, fn)
                else:
                    P.op("tensor", None, fn)
            for kc in range(NKC):
                P.wait("scalar", SV, op_evac_v[kc])
                P.wait("scalar", TM, op_mm_v[NKC - 1])
                P.wait("scalar", GP, op_load_v[NKC - 1])
                if kc >= 1:
                    P.wait("scalar", TM, mmq[(tci, kc - 1)])
                v_sq = P.op("scalar", SA, lambda e, kc=kc, tci=tci: e.activation(
                    out=fr(SQ3, 512), in_=fa(XR + kc * TL + tci * 512, 512), func=AF.Square))
                P.wait("tensor", SA, v_sq)
                fn = lambda e, kc=kc: e.matmul(
                    out=PS[7][0:1, 0:512], lhsT=onest, rhs=fr(SQ3, 512),
                    start=(kc == 0), stop=(kc == NKC - 1))
                mmq[(tci, kc)] = P.op("tensor", TM, fn)
            P.wait("vector", TM, mmq[(tci, NKC - 1)])
            P.wait("vector", TM, v_sum)
            P.op("vector", SV, lambda e, tci=tci: e.tensor_scalar(
                out=fo(MRBf, tci * 512, 512, parts=1), in0=PS[6][0:1, 0:512],
                scalar1=1.0 / H, scalar2=None, op0=ALU.mult))
            v_rowread = P.op("vector", SV, lambda e, tci=tci: e.tensor_scalar(
                out=fo(RB1f, tci * 512, 512, parts=1), in0=PS[7][0:1, 0:512],
                scalar1=1.0 / H, scalar2=None, op0=ALU.mult))
            P.op("vector", SV, lambda e, tci=tci: e.tensor_mul(
                out=fo(T1f, tci * 512, 512, parts=1), in0=fo(MRBf, tci * 512, 512, parts=1),
                in1=fo(MRBf, tci * 512, 512, parts=1)))
            v_var = P.op("vector", SV, lambda e, tci=tci: e.tensor_sub(
                out=fo(RB1f, tci * 512, 512, parts=1), in0=fo(RB1f, tci * 512, 512, parts=1),
                in1=fo(T1f, tci * 512, 512, parts=1)))
            P.wait("scalar", SV, v_var)
            P.op("scalar", SA, lambda e, tci=tci: e.activation(
                out=fo(RB1f, tci * 512, 512, parts=1), in_=fo(RB1f, tci * 512, 512, parts=1),
                func=AF.Ln, bias=cB[0:1, 0:1]))
            v_rs = P.op("scalar", SA, lambda e, tci=tci: e.activation(
                out=fo(RB1f, tci * 512, 512, parts=1), in_=fo(RB1f, tci * 512, 512, parts=1),
                func=AF.Exp, scale=-0.5))
            P.wait("vector", SA, v_rs)
            v_mr = P.op("vector", SV, lambda e, tci=tci: e.tensor_mul(
                out=fo(MRBf, tci * 512, 512, parts=1), in0=fo(MRBf, tci * 512, 512, parts=1),
                in1=fo(RB1f, tci * 512, 512, parts=1)))
        v_smr = P.op("vector", SV, lambda e: e.tensor_reduce(
            out=fo(T1f, 0, 1, parts=1), in_=fo(MRBf, 0, TL, parts=1), axis=mybir.AxisListType.X, op=ALU.add))
        P.wait("sync", SV, v_smr)
        P.dma("sync", LD, rowb[0:1, 0:TL], fo(RB1f, 0, TL, parts=1))
        P.dma("sync", LD, rowb[0:1, TL:2 * TL], fo(MRBf, 0, TL, parts=1))
        v_w = P.dma("sync", LD, rowb[0:1, 2 * TL:2 * TL + 1], fo(T1f, 0, 1, parts=1))
        P.wait("sync", LD, v_w)
        P.dma("sync", LD, fo(RB1f, 0, TL), rowb[0:1, 0:TL].to_broadcast((128, TL)))
        P.dma("sync", LD, smrB, rowb[0:1, 2 * TL:2 * TL + 1].to_broadcast((128, 1)))
        v_bc = P.dma("sync", LD, fo(MRBf, 0, TL), rowb[0:1, TL:2 * TL].to_broadcast((128, TL)))
        P.wait("vector", LD, v_bc)
        xln_v = [None] * NKC
        for kc in range(NKC):
            P.op("vector", SV, lambda e, kc=kc: e.tensor_mul(
                out=fo(T1f, 0, TL), in0=fa(XR + kc * TL, TL), in1=fo(RB1f, 0, TL)))
            P.op("vector", SV, lambda e, kc=kc: e.tensor_reduce(
                out=gsum[:, kc:kc + 1], in_=fo(T1f, 0, TL), axis=mybir.AxisListType.X, op=ALU.add))
            xln_v[kc] = P.op("vector", SV, lambda e, kc=kc: e.tensor_sub(
                out=ba(kc * TL, TL), in0=fo(T1f, 0, TL), in1=fo(MRBf, 0, TL)))
        v_gl = P.op("vector", SV, lambda e: e.tensor_scalar(
            out=gsum, in0=gsum, scalar1=smrB[:, 0:1], scalar2=None, op0=ALU.subtract))
        P.wait("sync", SV, v_gl)
        v_gst = P.dma("sync", LD, gst, gsum)
        P.wait("gpsimd", LD, v_gst)
        v_cc_g = P.cc("gpsimd", GP, "AllGather", gst, gg)
        P.wait("sync", GP, v_cc_g)
        v_ggr = None
        for r in range(4):
            v_ggr = P.dma("sync", LD, ggr[:, r * NKC:(r + 1) * NKC], gg[r * 128:(r + 1) * 128, :])
        P.wait("vector", LD, v_ggr)
        P.op("vector", SV, lambda e: e.tensor_add(out=m128, in0=ggr[:, 0:NKC], in1=ggr[:, NKC:2 * NKC]))
        P.op("vector", SV, lambda e: e.tensor_add(out=m128, in0=m128, in1=ggr[:, 2 * NKC:3 * NKC]))
        P.op("vector", SV, lambda e: e.tensor_add(out=m128, in0=m128, in1=ggr[:, 3 * NKC:4 * NKC]))
        v_m128 = P.op("vector", SV, lambda e: e.tensor_scalar(
            out=m128r, in0=m128, scalar1=1.0 / L, scalar2=None, op0=ALU.mult))
        P.wait("tensor", SV, v_m128)
        for kc in range(NKC):
            fn = lambda e, kc=kc: e.matmul(
                out=PS[4][0:1, 0:6], lhsT=m128r[:, kc:kc + 1], rhs=gws[:, kc * 6:(kc + 1) * 6],
                start=(kc == 0), stop=(kc == NKC - 1))
            if kc == NKC - 1:
                v_glog = P.op("tensor", TM, fn)
            else:
                P.op("tensor", None, fn)
        P.wait("vector", TM, v_glog)
        P.op("vector", SV, lambda e: e.memset(ls8, -3.0e38))
        P.op("vector", SV, lambda e: e.tensor_copy(out=ls8[:, 0:6], in_=PS[4][0:1, 0:6]))
        P.op("vector", SV, lambda e: e.max(out=mx8, in_=ls8))
        P.op("vector", SV, lambda e: e.tensor_scalar(
            out=eq8, in0=ls8, scalar1=mx8[:, 0:1], scalar2=None, op0=ALU.is_equal))
        P.op("vector", SV, lambda e: e.tensor_mul(out=eq8, in0=eq8, in1=iota8s))
        P.op("vector", SV, lambda e: e.tensor_reduce(
            out=grow[:, 2:3], in_=eq8, axis=mybir.AxisListType.X, op=ALU.add))
        P.op("vector", SV, lambda e: e.tensor_scalar(
            out=eq8, in0=ls8, scalar1=mx8[:, 1:2], scalar2=None, op0=ALU.is_equal))
        P.op("vector", SV, lambda e: e.tensor_mul(out=eq8, in0=eq8, in1=iota8s))
        P.op("vector", SV, lambda e: e.tensor_reduce(
            out=grow[:, 3:4], in_=eq8, axis=mybir.AxisListType.X, op=ALU.add))
        v_d01 = P.op("vector", SV, lambda e: e.tensor_sub(out=gtmp[:, 0:1], in0=mx8[:, 0:1], in1=mx8[:, 1:2]))
        P.wait("scalar", SV, v_d01)
        v_w0 = P.op("scalar", SA, lambda e: e.activation(out=grow[:, 0:1], in_=gtmp[:, 0:1], func=AF.Sigmoid))
        P.wait("vector", SA, v_w0)
        v_grow = P.op("vector", SV, lambda e: e.tensor_scalar(
            out=grow[:, 1:2], in0=grow[:, 0:1], scalar1=-1.0, scalar2=1.0, op0=ALU.mult, op1=ALU.add))
        P.wait("sync", SV, v_grow)
        v_w = P.dma("sync", LD, wrow, grow)
        P.wait("sync", LD, v_w)
        v_wB = P.dma("sync", LD, wB, wrow.to_broadcast((128, 8)))
        P.wait("vector", LD, v_wB)
        v_ecol = None
        for j, (src, mul) in enumerate(((2, 2048.0), (2, 1408.0), (2, 128.0),
                                        (3, 2048.0), (3, 1408.0), (3, 128.0))):
            v_ecol = P.op("vector", SV, lambda e, j=j, src=src, mul=mul: e.tensor_scalar(
                out=ecol[:, j:j + 1], in0=wB[:, src:src + 1], scalar1=mul, scalar2=None, op0=ALU.mult))

        _bg_prev = [None]

        def bias_gather(col, table, dst_ap):
            if _bg_prev[0] is not None:
                P.wait("vector", GP, _bg_prev[0])
            P.op("vector", SV, lambda e, col=col: e.tensor_add(
                out=idxsf, in0=iotaf, in1=ecol[:, col:col + 1]))
            v2 = P.op("vector", SV, lambda e: e.tensor_copy(out=idxtb, in_=idxsf))
            P.wait("gpsimd", SV, v2)
            _bg_prev[0] = P.idma("gpsimd", GP, dst_ap, table, idxtb)
            return _bg_prev[0]

        P.wait("vector", SV, v_ecol)
        bias_gather(2, b1fm, b1g[:, 0:NFC])
        bias_gather(5, b1fm, b1g[:, NFC:2 * NFC])
        bias_gather(2, b2fm, b2g[:, 0:NKC])
        v_bg = bias_gather(5, b2fm, b2g[:, NKC:2 * NKC])
        P.wait("vector", GP, v_bg)
        P.op("vector", SV, lambda e: e.tensor_scalar(
            out=b2c, in0=b2g[:, 0:NKC], scalar1=wB[:, 0:1], scalar2=None, op0=ALU.mult))
        v_b2c = P.op("vector", SV, lambda e: e.scalar_tensor_tensor(
            out=b2c, in0=b2g[:, NKC:2 * NKC], scalar=wB[:, 1:2], in1=b2c,
            op0=ALU.mult, op1=ALU.add))

        moe_gv = []

        def moe_idx(col, addi):
            P.op("vector", SV, lambda e, col=col: e.tensor_add(
                out=idxsf, in0=iotaf, in1=ecol[:, col:col + 1]))
            if addi:
                P.op("vector", SV, lambda e, addi=addi: e.tensor_scalar(
                    out=idxsf, in0=idxsf, scalar1=float(addi), scalar2=None, op0=ALU.add))
            slot = moe_idx.c % 16
            if moe_idx.c >= 16:
                P.wait("vector", GP, moe_gv[moe_idx.c - 16])
            moe_idx.c += 1
            v = P.op("vector", SV, lambda e, slot=slot: e.tensor_copy(
                out=idxm[:, slot:slot + 1], in_=idxsf))
            return v, idxm[:, slot:slot + 1]
        moe_idx.c = 0

        ff_evac = {}
        gel_all = {}
        gel_scaled_v = {}
        h1_last_mm = [None, None]
        ff_last_mm = [None, None]

        for s in range(2):
            ec0 = 0 if s == 0 else 3
            v_w1g = None
            for kc in range(NKC):
                vi, iap = moe_idx(ec0 + 0, kc * 128)
                P.wait("gpsimd", SV, vi)
                if s == 0:
                    P.wait("gpsimd", TM, op_mm_v[NKC - 1])
                else:
                    P.wait("gpsimd", TM, ff_last_mm[0])
                v_w1g = P.idma("gpsimd", GP, fo(W1B, kc * DFF, DFF), w1f, iap)
                moe_gv.append(v_w1g)
            P.wait("tensor", GP, v_w1g)
            P.wait("tensor", SV, xln_v[NKC - 1])
            if s == 0:
                P.wait("tensor", SV, op_evac_v[NKC - 1])
            for fc in range(NFC):
                for tci in range(2):
                    g = fc * 2 + tci
                    bk = g % 4
                    if g >= 4:
                        P.wait("tensor", SA, gel_all[(s, g - 4)])
                    elif s == 1:
                        P.wait("tensor", SA, gel_all[(0, NFC * 2 - 4 + g)])
                    for kc in range(NKC):
                        fn = lambda e, fc=fc, tci=tci, kc=kc, bk=bk: e.matmul(
                            out=PS[bk][:, 0:512],
                            lhsT=fo(W1B, kc * DFF + fc * 128, 128),
                            rhs=ba(kc * TL + tci * 512, 512),
                            start=(kc == 0), stop=(kc == NKC - 1))
                        if kc == NKC - 1:
                            v_h1 = P.op("tensor", TM, fn)
                        else:
                            P.op("tensor", None, fn)
                    if fc == NFC - 1 and tci == 1:
                        h1_last_mm[s] = v_h1
                    P.wait("scalar", TM, v_h1)
                    if s == 1 and g < 4:
                        P.wait("scalar", TM, ff_last_mm[0])
                    gel_all[(s, g)] = P.op("scalar", SA, lambda e, fc=fc, tci=tci, bk=bk, s=s: e.activation(
                        out=fo(GB, fc * TL + tci * 512, 512), in_=PS[bk][:, 0:512],
                        func=AF.Gelu, bias=b1g[:, s * NFC + fc:s * NFC + fc + 1]))
                    P.wait("vector", SA, gel_all[(s, g)])
                    gel_scaled_v[(s, g)] = P.op("vector", SV, lambda e, fc=fc, tci=tci, s=s: e.tensor_scalar(
                        out=fo(GB, fc * TL + tci * 512, 512), in0=fo(GB, fc * TL + tci * 512, 512),
                        scalar1=wB[:, s:s + 1], scalar2=None, op0=ALU.mult))
            v_w2g = None
            for fc in range(NFC):
                vi, iap = moe_idx(ec0 + 1, fc * 128)
                P.wait("gpsimd", SV, vi)
                P.wait("gpsimd", TM, h1_last_mm[s])
                v_w2g = P.idma("gpsimd", GP, fo(W1B, fc * H, H), w2f, iap)
                moe_gv.append(v_w2g)
            P.wait("tensor", GP, v_w2g)
            P.wait("vector", SV, v_b2c)
            for oc in range(NKC):
                for tci in range(2):
                    g = oc * 2 + tci
                    bk = 4 + g % 4
                    if g >= 4:
                        P.wait("tensor", SV, ff_evac[(s, g - 4)])
                    elif s == 1:
                        P.wait("tensor", SV, ff_evac[(0, NKC * 2 - 4 + g)])
                    for fc in range(NFC):
                        P.wait("tensor", SV, gel_scaled_v[(s, fc * 2 + tci)])
                        fn = lambda e, oc=oc, tci=tci, fc=fc, bk=bk: e.matmul(
                            out=PS[bk][:, 0:512],
                            lhsT=fo(W1B, fc * H + oc * 128, 128),
                            rhs=fo(GB, fc * TL + tci * 512, 512),
                            start=(fc == 0), stop=(fc == NFC - 1))
                        if fc == NFC - 1:
                            v_ff = P.op("tensor", TM, fn)
                        else:
                            P.op("tensor", None, fn)
                    if oc == NKC - 1 and tci == 1:
                        ff_last_mm[s] = v_ff
                    P.wait("vector", TM, v_ff)
                    if s == 0:
                        ff_evac[(s, g)] = P.op("vector", SV, lambda e, oc=oc, tci=tci, bk=bk:
                                               e.scalar_tensor_tensor(
                                                   out=fo(X2B, oc * TL + tci * 512, 512),
                                                   in0=PS[bk][:, 0:512], scalar=b2c[:, oc:oc + 1],
                                                   in1=ba(oc * TL + tci * 512, 512),
                                                   op0=ALU.add, op1=ALU.add))
                    else:
                        ff_evac[(s, g)] = P.op("vector", SV, lambda e, oc=oc, tci=tci, bk=bk:
                                               e.tensor_add(
                                                   out=fo(X2B, oc * TL + tci * 512, 512),
                                                   in0=PS[bk][:, 0:512],
                                                   in1=fo(X2B, oc * TL + tci * 512, 512)))

        P.wait("tensor", SV, ff_evac[(1, NKC * 2 - 1)])
        mmq2 = {}
        v_mr2 = None
        v_rowread2 = None
        for tci in range(2):
            if v_rowread2 is not None:
                P.wait("tensor", SV, v_rowread2)
            for kc in range(NKC):
                fn = lambda e, kc=kc, tci=tci: e.matmul(
                    out=PS[0][0:1, 0:512], lhsT=onesb[:], rhs=fo(X2B, kc * TL + tci * 512, 512),
                    start=(kc == 0), stop=(kc == NKC - 1))
                if kc == NKC - 1:
                    v_sum2 = P.op("tensor", TM, fn)
                else:
                    P.op("tensor", None, fn)
            for kc in range(NKC):
                P.wait("scalar", TM, v_sum2 if kc == 0 else mmq2[(tci, kc - 1)])
                v_sq2 = P.op("scalar", SA, lambda e, kc=kc, tci=tci: e.activation(
                    out=ba(0, 512), in_=fo(X2B, kc * TL + tci * 512, 512), func=AF.Square))
                P.wait("tensor", SA, v_sq2)
                fn = lambda e, kc=kc: e.matmul(
                    out=PS[1][0:1, 0:512], lhsT=onesb[:], rhs=ba(0, 512),
                    start=(kc == 0), stop=(kc == NKC - 1))
                mmq2[(tci, kc)] = P.op("tensor", TM, fn)
            P.wait("vector", TM, mmq2[(tci, NKC - 1)])
            P.wait("vector", TM, v_sum2)
            P.op("vector", SV, lambda e, tci=tci: e.tensor_scalar(
                out=fo(MRBf, tci * 512, 512, parts=1), in0=PS[0][0:1, 0:512],
                scalar1=1.0 / H, scalar2=None, op0=ALU.mult))
            v_rowread2 = P.op("vector", SV, lambda e, tci=tci: e.tensor_scalar(
                out=fo(RB1f, tci * 512, 512, parts=1), in0=PS[1][0:1, 0:512],
                scalar1=1.0 / H, scalar2=None, op0=ALU.mult))
            P.op("vector", SV, lambda e, tci=tci: e.tensor_mul(
                out=fo(T1f, tci * 512, 512, parts=1), in0=fo(MRBf, tci * 512, 512, parts=1),
                in1=fo(MRBf, tci * 512, 512, parts=1)))
            v_var2 = P.op("vector", SV, lambda e, tci=tci: e.tensor_sub(
                out=fo(RB1f, tci * 512, 512, parts=1), in0=fo(RB1f, tci * 512, 512, parts=1),
                in1=fo(T1f, tci * 512, 512, parts=1)))
            P.wait("scalar", SV, v_var2)
            P.op("scalar", SA, lambda e, tci=tci: e.activation(
                out=fo(RB1f, tci * 512, 512, parts=1), in_=fo(RB1f, tci * 512, 512, parts=1),
                func=AF.Ln, bias=cB[0:1, 0:1]))
            v_rs2 = P.op("scalar", SA, lambda e, tci=tci: e.activation(
                out=fo(RB1f, tci * 512, 512, parts=1), in_=fo(RB1f, tci * 512, 512, parts=1),
                func=AF.Exp, scale=-0.5))
            P.wait("vector", SA, v_rs2)
            v_mr2 = P.op("vector", SV, lambda e, tci=tci: e.tensor_mul(
                out=fo(MRBf, tci * 512, 512, parts=1), in0=fo(MRBf, tci * 512, 512, parts=1),
                in1=fo(RB1f, tci * 512, 512, parts=1)))
        P.wait("sync", SV, v_mr2)
        P.dma("sync", LD, rowb[0:1, 0:TL], fo(RB1f, 0, TL, parts=1))
        v_w = P.dma("sync", LD, rowb[0:1, TL:2 * TL], fo(MRBf, 0, TL, parts=1))
        P.wait("sync", LD, v_w)
        P.dma("sync", LD, fo(RB1f, 0, TL), rowb[0:1, 0:TL].to_broadcast((128, TL)))
        v_bc2 = P.dma("sync", LD, fo(MRBf, 0, TL), rowb[0:1, TL:2 * TL].to_broadcast((128, TL)))
        P.wait("vector", LD, v_bc2)
        outd = {}
        for kc in range(NKC):
            P.op("vector", SV, lambda e, kc=kc: e.tensor_mul(
                out=fo(T1f, 0, TL), in0=fo(X2B, kc * TL, TL), in1=fo(RB1f, 0, TL)))
            tgt = OUTS[0] if kc % 2 == 0 else OUTS[0] + 512
            tgt = ZYT[kc % 2]
            if kc >= 2:
                P.wait("vector", LD, outd[kc - 2])
            v_ap = P.op("vector", SV, lambda e, kc=kc, tgt=tgt: e.tensor_sub(
                out=fa(tgt, TL), in0=fo(T1f, 0, TL), in1=fo(MRBf, 0, TL)))
            P.wait("sync", SV, v_ap)
            outd[kc] = P.dma("sync", LD, out_fm[kc * 128:(kc + 1) * 128, :], fa(tgt, TL))

        P.finish()

        @block.sync
        def _(e):
            for f in P.q["sync"]:
                f(e)

        @block.gpsimd
        def _(e):
            for f in P.q["gpsimd"]:
                f(e)

        @block.scalar
        def _(e):
            for f in P.q["scalar"]:
                f(e)

        @block.vector
        def _(e):
            for f in P.q["vector"]:
                f(e)

        @block.tensor
        def _(e):
            for f in P.q["tensor"]:
                f(e)

    return nc


_NC_CACHE = None


def kernel(L0_attn_x, L0_attn_norm_w, L0_attn_in_proj_w, L0_attn_conv_w, L0_attn_conv_b,
           L0_attn_x_proj_w, L0_attn_dt_proj_w, L0_attn_dt_proj_b, L0_attn_A_log, L0_attn_D,
           L0_attn_out_proj_w, L0_ffn_gate_W, L0_ffn_W1, L0_ffn_b1, L0_ffn_W2, L0_ffn_b2):
    global _NC_CACHE
    f32 = np.float32
    x = np.asarray(L0_attn_x, f32)
    in_proj_w = np.asarray(L0_attn_in_proj_w, f32)
    conv_w = np.asarray(L0_attn_conv_w, f32)
    out_proj_w = np.asarray(L0_attn_out_proj_w, f32)
    x_proj_w = np.asarray(L0_attn_x_proj_w, f32)
    dt_proj_w = np.asarray(L0_attn_dt_proj_w, f32)
    dt_proj_b = np.asarray(L0_attn_dt_proj_b, f32)
    A_log = np.asarray(L0_attn_A_log, f32)
    D = np.asarray(L0_attn_D, f32)
    norm_w = np.asarray(L0_attn_norm_w, f32)
    conv_b = np.asarray(L0_attn_conv_b, f32)
    gate_W = np.asarray(L0_ffn_gate_W, f32)
    W1 = np.asarray(L0_ffn_W1, f32)
    b1 = np.asarray(L0_ffn_b1, f32)
    W2 = np.asarray(L0_ffn_W2, f32)
    b2 = np.asarray(L0_ffn_b2, f32)

    shared = dict(
        wi=np.ascontiguousarray(in_proj_w.reshape(32, 128, NKC, 128).transpose(0, 3, 2, 1)
                                .reshape(32 * 128, NKC * 128)),
        wc=np.ascontiguousarray(conv_w.reshape(4, NKC, 128, 16, 128).transpose(3, 2, 0, 1, 4)
                                .reshape(16 * 128, 4 * NKC * 128)),
        wo=np.ascontiguousarray(out_proj_w.reshape(16, 128, NKC, 128).transpose(0, 3, 2, 1)
                                .reshape(16 * 128, NKC * 128)),
        xpv=np.ascontiguousarray(x_proj_w.reshape(96, NKC, 128).transpose(2, 1, 0)
                                 .reshape(128, NKC * 96)),
        nw=np.ascontiguousarray(norm_w.reshape(NKC, 128).T),
        cbv=np.ascontiguousarray(conv_b.reshape(NKC, 128).T),
        gwv=np.ascontiguousarray(gate_W.reshape(6, NKC, 128).transpose(2, 1, 0)
                                 .reshape(128, NKC * 6)),
        w1f=np.ascontiguousarray(W1.reshape(6 * H, DFF)),
        w2f=np.ascontiguousarray(W2.reshape(6 * DFF, H)),
        b1fm=np.ascontiguousarray(b1.reshape(6, NFC, 128).transpose(0, 2, 1).reshape(6 * 128, NFC)),
        b2fm=np.ascontiguousarray(b2.reshape(6, NKC, 128).transpose(0, 2, 1).reshape(6 * 128, NKC)),
        identi=np.eye(128, dtype=f32),
        onesf=np.ones((128, 1), f32),
        iota8=np.arange(8, dtype=f32).reshape(1, 8),
        cvals=np.array([[EPS, 0.0]], f32),
    )

    in_maps = []
    for c in range(8):
        b, q = divmod(c, 4)
        t0 = q * TL
        xs = np.zeros((H, TC), f32)
        lo = max(0, t0 - HALO)
        xs[:, HALO - (t0 - lo):] = x[b, lo:t0 + TL, :].T
        m = dict(shared)
        m["x_fm"] = np.ascontiguousarray(xs)
        m["dtw"] = np.ascontiguousarray(dt_proj_w[q * 512:(q + 1) * 512, :].T)
        m["dtb"] = np.ascontiguousarray(dt_proj_b[q * 512:(q + 1) * 512].reshape(4, 128).T)
        m["alog"] = np.ascontiguousarray(A_log[q * 512:(q + 1) * 512].reshape(4, 128, NST)
                                         .transpose(1, 0, 2).reshape(128, 4 * NST))
        m["dvec"] = np.ascontiguousarray(D[q * 512:(q + 1) * 512].reshape(4, 128).T)
        m["qoff"] = np.array([[q * 512, q]], np.int32)
        in_maps.append(m)

    if _NC_CACHE is None:
        _NC_CACHE = build_nc()
    res = run_bass_kernel_spmd(_NC_CACHE, in_maps, core_ids=list(range(8)))

    out = np.empty((2, L, H), f32)
    for c in range(8):
        b, q = divmod(c, 4)
        out[b, q * TL:(q + 1) * TL, :] = res.results[c]["out_fm"].T
    return out


# version 11
# speedup vs baseline: 1.0937x; 1.0203x over previous
import contextlib
import numpy as np

import concourse.bass as bass
import concourse.mybir as mybir
from concourse.bass_utils import run_bass_kernel_spmd

F32 = mybir.dt.float32
F32R = mybir.dt.float32r
BF16 = mybir.dt.bfloat16
I32 = mybir.dt.int32
AF = mybir.ActivationFunctionType
ALU = mybir.AluOpType

H = 2048
L = 4096
TL = 1024
HALO = 4
TC = TL + HALO
NKC = 16
DTR = 64
NST = 16
DFF = 1408
NFC = 11
EPS = 1e-5

ENGS = ("sync", "gpsimd", "scalar", "vector", "tensor")
GROUPS = [[0, 1, 2, 3], [4, 5, 6, 7]]
FASZ = 41984


class Sem:
    def __init__(self, h):
        self.h = h
        self.v = 0

    def bump(self, k):
        self.v += k
        return self.v


QDEPTH = 2


class Plan:

    def __init__(self, dnsems, rotsems, esems, ccsem):
        self.ccs = ccsem
        self.q = {e: [] for e in ENGS}
        self._lastwait = {}
        self.dn = dnsems
        self.rots = rotsems
        self.esem = esems
        self.pending = {"sync": [], "gpsimd": []}
        self.echoed = {"sync": 0, "gpsimd": 0}
        self.issued = {"sync": 0, "gpsimd": 0}
        self.gwait = {"sync": [], "gpsimd": []}

    def _emit_echo(self, qn):
        h, v = self.pending[qn].pop(0)
        self.q[qn].append(lambda e, h=h, v=v: e.wait_ge(h, v))
        dn = self.dn[qn]
        self.q[qn].append(lambda e, s=dn.h: e.sem_inc(s, 1))
        dn.bump(1)
        self.echoed[qn] += 1

    def _flush_to(self, qn, val):
        while self.echoed[qn] < val and self.pending[qn]:
            self._emit_echo(qn)

    def _drain_gwaits(self, qn):
        for fn in self.gwait[qn]:
            self.q[qn].append(fn)
        self.gwait[qn] = []

    def _pre_dma(self, qn):
        if len(self.pending[qn]) >= QDEPTH:
            self._emit_echo(qn)
        self._drain_gwaits(qn)

    def dma(self, eng, sem, out, in_, **kw):
        qn = eng
        self._pre_dma(qn)
        rot = self.rots[qn][self.issued[qn] % 2]
        self.issued[qn] += 1
        v = rot.bump(16)
        self.q[qn].append(lambda e, out=out, in_=in_, kw=dict(kw), s=rot.h:
                          e.dma_start(out=out, in_=in_, **kw).then_inc(s, 16))
        self.pending[qn].append((rot.h, v))
        return self.issued[qn]

    def idma(self, eng, sem, out, in_, idx, element_offset=0):
        qn = eng
        self._pre_dma(qn)
        rot = self.rots[qn][self.issued[qn] % 2]
        self.issued[qn] += 1
        v = rot.bump(16)
        self.q[qn].append(lambda e, out=out, in_=in_, idx=idx, eo=element_offset, s=rot.h:
                          e.indirect_dma_start(out=out, out_offset=None, in_=in_,
                                               in_offset=bass.IndirectOffsetOnAxis(ap=idx, axis=0),
                                               element_offset=eo).then_inc(s, 16))
        self.pending[qn].append((rot.h, v))
        return self.issued[qn]

    def cc(self, eng, sem, kind, in_ap, out_ap):
        qn = eng
        self._pre_dma(qn)
        rot = self.ccs
        self.issued[qn] += 1
        v = rot.bump(1)
        self.q[qn].append(lambda e, k=kind, i=in_ap, o=out_ap, s=rot.h:
                          e.collective_compute(k, ALU.bypass, replica_groups=GROUPS,
                                               ins=[i.opt()], outs=[o.opt()]).then_inc(s, 1))
        self.pending[qn].append((rot.h, v))
        return self.issued[qn]

    def op(self, eng, sem, fn):
        if eng == "gpsimd":
            self._drain_gwaits("gpsimd")
        if eng in ("vector", "scalar", "gpsimd"):
            es = self.esem[eng]
            if es.v > 0:
                self.q[eng].append(lambda e, s=es.h, v=es.v: e.wait_ge(s, v))
            v = es.bump(1)
            self.q[eng].append(lambda e, fn=fn, s=es.h: fn(e).then_inc(s, 1))
            return v
        if sem is None:
            self.q[eng].append(fn)
            return None
        v = sem.bump(1)
        self.q[eng].append(lambda e, fn=fn, s=sem.h: fn(e).then_inc(s, 1))
        return v

    def wait(self, eng, sem, val):
        if val is None or val <= 0:
            return
        for qn in ("sync", "gpsimd"):
            if sem is self.dn[qn]:
                self._flush_to(qn, val)
                break
        key = (eng, id(sem))
        if self._lastwait.get(key, -1) >= val:
            return
        self._lastwait[key] = val
        fn = lambda e, s=sem.h, v=val: e.wait_ge(s, v)
        if eng in ("sync", "gpsimd"):
            self.gwait[eng].append(fn)
        else:
            self.q[eng].append(fn)

    def finish(self):
        for qn in ("sync", "gpsimd"):
            while self.pending[qn]:
                self._emit_echo(qn)
            self._drain_gwaits(qn)


def build_nc():
    nc = bass.Bass("TRN2", target_bir_lowering=False)

    def din(name, shape, dt):
        return nc.dram_tensor(name, shape, dt, kind="ExternalInput").ap()

    def dint(name, shape, dt):
        return nc.dram_tensor(name, shape, dt).ap()

    x_fm = din("x_fm", [H, TC], F32R)
    wi = din("wi", [32 * 128, NKC * 128], F32R)
    wc = din("wc", [16 * 128, 4 * NKC * 128], F32R)
    wo = din("wo", [16 * 128, NKC * 128], F32R)
    xpv = din("xpv", [128, NKC * 96], F32R)
    dtw = din("dtw", [DTR, 512], F32)
    dtb = din("dtb", [128, 4], F32)
    alog = din("alog", [128, 4 * NST], F32)
    dvec = din("dvec", [128, 4], F32)
    nw = din("nw", [128, NKC], F32)
    cbv = din("cbv", [128, NKC], F32)
    gwv = din("gwv", [128, NKC * 6], F32)
    w1f = din("w1f", [6 * H, DFF], F32)
    w2f = din("w2f", [6 * DFF, H], F32)
    b1fm = din("b1fm", [6 * 128, NFC], F32)
    b2fm = din("b2fm", [6 * 128, NKC], F32)
    identi = din("identi", [128, 128], F32)
    onesf = din("onesf", [128, 1], F32)
    iota8 = din("iota8", [1, 8], F32)
    qoff = din("qoff", [1, 2], I32)
    cvals = din("cvals", [1, 2], F32)

    out_fm = nc.dram_tensor("out_fm", [H, TL], F32, kind="ExternalOutput").ap()

    zdram = dint("zdram", [H, TL], F32)
    xcst = dint("xcst", [H, TL], F32)
    dbcst = dint("dbcst", [96, TL], F32)
    xcg = dint("xcg", [4 * H, TL], F32)
    dbcg = dint("dbcg", [4 * 96, TL], F32)
    bmcmd = dint("bmcmd", [32, L], F32R)
    yst = dint("yst", [512, L], F32)
    yg = dint("yg", [H, L], F32)
    ygv = yg.rearrange("c (r t) -> (c r) t", t=TL)
    gst = dint("gst", [128, NKC], F32)
    gg = dint("gg", [512, NKC], F32)
    rowb = dint("rowb", [1, 2 * TC], F32)
    wrow = dint("wrow", [1, 8], F32)

    with contextlib.ExitStack() as ctx:
        sbt = lambda n, s, t: ctx.enter_context(nc.sbuf_tensor(n, s, t))
        ARENA = sbt("ARENA", [128, FASZ * 4], mybir.dt.uint8)
        _abase = nc.lookup_mloc(ARENA).addr

        def alias(name, shape, dtype, byte_off):
            h = nc.virtual_sbuf_tensor(name, shape, dtype)
            nc._tensor(h.name, h.shape, h.dtype, type="SB")
            m = nc.lookup_mloc(h)
            m.allocated = True
            m.addr = _abase + byte_off
            return h

        FRt = alias("FRt", [128, 32896], F32R, 0)
        SQRt = alias("SQRt", [128, 1028], F32R, 32896 * 4)
        RBRt = alias("RBRt", [128, 1028], F32, 33928 * 4)
        WINRt = alias("WINRt", [128, 4976], F32R, 34960 * 4)
        ZTRt = alias("ZTRt", [128, 2048], F32, 39936 * 4)
        ABf = alias("ABf", [128, 1024], F32, 16384 * 4)
        BTf = alias("BTf", [128, 1024], F32, 17408 * 4)
        STf = alias("STf", [128, 1024], F32, 18432 * 4)
        RB1f = alias("RB1f", [128, 1024], F32, 32768 * 4)
        MRBf = alias("MRBf", [128, 1024], F32, 33792 * 4)
        T1f = alias("T1f", [128, 1024], F32, 34816 * 4)
        XPWr = alias("XPWr", [128, 1536], F32R, 39936 * 4)
        GB = alias("GB", [128, 11264], BF16, 0)
        X2B = alias("X2B", [128, 16384], BF16, 16384 * 2)
        W1B = alias("W1B", [128, 22528], BF16, 34816 * 2)

        BA = sbt("BA", [128, 16384], BF16)
        SM = sbt("SM", [128, 1088], F32)
        SR = sbt("SR", [128, 768], F32R)
        SI = sbt("SI", [128, 64], I32)
        R8 = sbt("R8", [1, 64], F32)
        onesb = sbt("onesb", [128, 1], BF16)
        dtbs = SM[:, 0:4]
        alogs = SM[:, 4:68]
        aneg = SM[:, 68:132]
        dvs = SM[:, 132:136]
        nws = SM[:, 136:152]
        cbs = SM[:, 152:168]
        gsum = SM[:, 168:184]
        ggr = SM[:, 184:248]
        m128 = SM[:, 248:264]
        smrB = SM[:, 264:265]
        wB = SM[:, 265:273]
        ecol = SM[:, 273:279]
        b1g = SM[:, 279:301]
        b2g = SM[:, 301:333]
        b2c = SM[:, 333:349]
        carry = SM[:, 349:413]
        qcolf = SM[:, 413:415]
        cB = SM[:, 415:417]
        iotaf = SM[:, 417:418]
        iota4f = SM[:, 418:419]
        idxsf = SM[:, 419:420]
        dxs = SM[:, 448:960]
        dtws = SR[0:64, 0:512]
        gws = SR[:, 512:608]
        idents = SR[:, 608:736]
        onest = SR[:, 736:737]
        m128r = SR[:, 737:753]
        qcoli = SI[:, 0:2]
        iotai = SI[:, 2:3]
        iota4i = SI[:, 3:4]
        idxg = SI[:, 4:20]
        idxy = SI[:, 20:36]
        idxm = SI[:, 36:52]
        idxtb = SI[:, 52:53]
        iota8s = R8[:, 0:8]
        ls8 = R8[:, 8:16]
        mx8 = R8[:, 16:24]
        eq8 = R8[:, 24:32]
        grow = R8[:, 32:40]
        gtmp = R8[:, 40:48]

        PS = [ctx.enter_context(nc.psum_tensor(f"ps{i}", [128, 512], F32)) for i in range(8)]

        LD = Sem(ctx.enter_context(nc.semaphore("LD")))
        GP = Sem(ctx.enter_context(nc.semaphore("GP")))
        SA = Sem(ctx.enter_context(nc.semaphore("SA")))
        SV = Sem(ctx.enter_context(nc.semaphore("SV")))
        TM = Sem(ctx.enter_context(nc.semaphore("TM")))
        GC = Sem(ctx.enter_context(nc.semaphore("GC")))
        LDa = Sem(ctx.enter_context(nc.semaphore("LDa")))
        LDb = Sem(ctx.enter_context(nc.semaphore("LDb")))
        GPa = Sem(ctx.enter_context(nc.semaphore("GPa")))
        GPb = Sem(ctx.enter_context(nc.semaphore("GPb")))
        CCS = Sem(ctx.enter_context(nc.semaphore("CCS")))

        block = ctx.enter_context(nc.Block())
        P = Plan({"sync": LD, "gpsimd": GP}, {"sync": [LDa, LDb], "gpsimd": [GPa, GPb]},
                 {"vector": SV, "scalar": SA, "gpsimd": GC, "tensor": TM}, CCS)

        REGIONS = ((0, 32896, FRt), (32896, 33924, SQRt), (33928, 34956, RBRt),
                   (34960, 39936, WINRt), (39936, 41984, ZTRt))

        def fa(off, ln, parts=128, p0=0):
            for s, e2, t in REGIONS:
                if s <= off and off + ln <= e2:
                    return t[p0:p0 + parts, off - s:off - s + ln]
            raise AssertionError((off, ln))

        fr = fa

        def faf(off, ln, parts=128, p0=0):
            return fa(off, ln, parts, p0).bitcast(F32)

        def fo(t, off, ln, parts=128, p0=0):
            return t[p0:p0 + parts, off:off + ln]

        def ba(off, ln, parts=128, p0=0):
            assert 0 <= off and off + ln <= 16384, (off, ln)
            return BA[p0:p0 + parts, off:off + ln]

        vx = [P.dma("sync", LD, fa(kc * TC, TC), x_fm[kc * 128:(kc + 1) * 128, :])
              for kc in range(NKC)]
        for dst, src in ((dtbs, dtb), (alogs, alog), (dvs, dvec), (nws, nw), (cbs, cbv),
                         (iota8s, iota8)):
            P.dma("sync", LD, dst[:], src)
        P.dma("sync", LD, qcoli, qoff.to_broadcast((128, 2)))
        v_consts = P.dma("sync", LD, cB, cvals.to_broadcast((128, 2)))
        P.dma("gpsimd", GP, dtws, dtw)
        P.dma("gpsimd", GP, gws, gwv)
        P.dma("gpsimd", GP, idents, identi)
        v_gp0 = P.dma("gpsimd", GP, onest, onesf)
        P.op("gpsimd", GC, lambda e: e.iota(iotai, pattern=[[0, 1]], base=0, channel_multiplier=1))
        v_iota = P.op("gpsimd", GC, lambda e: e.iota(iota4i, pattern=[[0, 1]], base=0, channel_multiplier=4))

        P.wait("vector", LD, v_consts)
        P.wait("vector", GC, v_iota)
        P.op("vector", SV, lambda e: e.memset(carry, 0.0))
        P.op("vector", SV, lambda e: e.memset(grow, 0.0))
        P.op("vector", SV, lambda e: e.memset(onesb[:], 1.0))
        P.op("vector", SV, lambda e: e.tensor_copy(out=iotaf, in_=iotai))
        P.op("vector", SV, lambda e: e.tensor_copy(out=iota4f, in_=iota4i))
        P.op("vector", SV, lambda e: e.tensor_copy(out=qcolf, in_=qcoli))
        P.wait("scalar", LD, v_consts)
        v_expal = P.op("scalar", SA, lambda e: e.activation(out=aneg, in_=alogs, func=AF.Exp))
        P.wait("vector", SA, v_expal)
        v_aneg = P.op("vector", SV, lambda e: e.tensor_scalar(
            out=aneg, in0=aneg, scalar1=-1.0, scalar2=None, op0=ALU.mult))

        XHN = 0
        XZ0 = 16448
        SQ = (32896, XZ0)
        RB = 33928
        WIN = (34960, 37008)
        ZT = (39936, 40960)
        chunks = ((0, HALO), (HALO, HALO + 512), (HALO + 512, TC))

        P.wait("tensor", GP, v_gp0)
        var_mm_v = [None] * NKC
        for kc in range(NKC):
            P.wait("scalar", LD, vx[kc])
            if kc >= 2:
                P.wait("scalar", TM, var_mm_v[kc - 2])
            v_sq = P.op("scalar", SA, lambda e, kc=kc: e.activation(
                out=fr(SQ[kc % 2], TC), in_=fa(XHN + kc * TC, TC), func=AF.Square))
            P.wait("tensor", SA, v_sq)
            for ci, (c0, c1) in enumerate(chunks):
                fn = lambda e, kc=kc, ci=ci, c0=c0, c1=c1: e.matmul(
                    out=PS[5 + ci][0:1, 0:(c1 - c0)], lhsT=onest,
                    rhs=fr(SQ[kc % 2] + c0, c1 - c0),
                    start=(kc == 0), stop=(kc == NKC - 1))
                if ci == 2:
                    var_mm_v[kc] = P.op("tensor", TM, fn)
                else:
                    P.op("tensor", None, fn)
        P.wait("scalar", TM, var_mm_v[NKC - 1])
        for ci, (c0, c1) in enumerate(chunks):
            P.op("scalar", SA, lambda e, ci=ci, c0=c0, c1=c1: e.activation(
                out=fa(RB + c0, c1 - c0, parts=1), in_=PS[5 + ci][0:1, 0:(c1 - c0)],
                func=AF.Ln, scale=1.0 / H, bias=cB[0:1, 0:1]))
            v_rr = P.op("scalar", SA, lambda e, c0=c0, c1=c1: e.activation(
                out=fa(RB + c0, c1 - c0, parts=1), in_=fa(RB + c0, c1 - c0, parts=1),
                func=AF.Exp, scale=-0.5))
        P.wait("sync", SA, v_rr)
        v_w = P.dma("sync", LD, rowb[0:1, 0:TC], fa(RB, TC, parts=1))
        P.wait("sync", LD, v_w)
        v_rstdB = P.dma("sync", LD, fa(RB, TC), rowb[0:1, 0:TC].to_broadcast((128, TC)))

        P.wait("vector", LD, v_rstdB)
        hn_v = [None] * NKC
        for kc in range(NKC):
            hn_v[kc] = P.op("vector", SV, lambda e, kc=kc: e.scalar_tensor_tensor(
                out=fr(XHN + kc * TC, TC), in0=fa(XHN + kc * TC, TC), scalar=nws[:, kc:kc + 1],
                in1=fa(RB, TC), op0=ALU.mult, op1=ALU.mult))
        v_hn_all = hn_v[-1]

        ip_load_v = [None] * 32
        ip_mm_v = [None] * 32
        ip_evac_v = [None] * 32
        z_dma_v = {}

        def ip_compute(oc):
            bk = 0 if oc % 2 == 0 else 3
            P.wait("tensor", GP, ip_load_v[oc])
            P.wait("tensor", SV, v_hn_all)
            if oc >= 2:
                P.wait("tensor", SA, ip_evac_v[oc - 2])
            wslot = WIN[oc % 2]
            cl = chunks if oc < 16 else chunks[1:]
            for ci, (c0, c1) in enumerate(cl):
                for kc in range(NKC):
                    fn = lambda e, kc=kc, bk=bk, ci=ci, c0=c0, c1=c1, wslot=wslot: e.matmul(
                        out=PS[bk + ci][:, 0:(c1 - c0)],
                        lhsT=fr(wslot + kc * 128, 128),
                        rhs=fr(XHN + kc * TC + c0, c1 - c0),
                        start=(kc == 0), stop=(kc == NKC - 1))
                    if ci == len(cl) - 1 and kc == NKC - 1:
                        ip_mm_v[oc] = P.op("tensor", TM, fn)
                    else:
                        P.op("tensor", None, fn)
            P.wait("scalar", TM, ip_mm_v[oc])
            if oc < 16:
                for ci, (c0, c1) in enumerate(cl):
                    ip_evac_v[oc] = P.op("scalar", SA, lambda e, oc=oc, bk=bk, ci=ci, c0=c0, c1=c1:
                                         e.activation(out=fr(XZ0 + oc * TC + c0, c1 - c0),
                                                      in_=PS[bk + ci][:, 0:(c1 - c0)], func=AF.Copy))
            else:
                zslot = ZT[oc % 2]
                if oc >= 18:
                    P.wait("scalar", LD, z_dma_v[oc - 2])
                for ci, (c0, c1) in enumerate(cl):
                    ip_evac_v[oc] = P.op("scalar", SA, lambda e, zslot=zslot, bk=bk, ci=ci, c0=c0, c1=c1:
                                         e.activation(out=fa(zslot + (c0 - HALO), c1 - c0),
                                                      in_=PS[bk + ci][:, 0:(c1 - c0)], func=AF.Copy))
                P.wait("sync", SA, ip_evac_v[oc])
                z_dma_v[oc] = P.dma("sync", LD, zdram[(oc - 16) * 128:(oc - 15) * 128, :],
                                    fa(zslot, TL))

        for i in range(34):
            if i >= 2:
                ip_compute(i - 2)
            if i < 32:
                if i >= 2:
                    P.wait("gpsimd", TM, ip_mm_v[i - 2])
                ip_load_v[i] = P.dma("gpsimd", GP, fr(WIN[i % 2], 2048), wi[i * 128:(i + 1) * 128, :])

        XCB = 0
        cv_steps = [(oc, k) for oc in range(16) for k in range(4)]
        cvk_mm = {}
        cv_evac_v = [None] * 16
        xc_dma_v = [None] * 16
        cv_load_v = {}
        cchunks = ((HALO, HALO + 512), (HALO + 512, TC))

        def cv_step_compute(j):
            oc, k = cv_steps[j]
            bk = 0 if oc % 2 == 0 else 2
            P.wait("tensor", GP, cv_load_v[(oc, k)])
            if k == 0:
                P.wait("tensor", SA, ip_evac_v[14])
                P.wait("tensor", SA, ip_evac_v[15])
                if oc >= 2:
                    P.wait("tensor", SA, cv_evac_v[oc - 2])
            wslot = WIN[j % 2]
            for ci, (c0, c1) in enumerate(cchunks):
                for kc in range(NKC):
                    fn = lambda e, k=k, kc=kc, bk=bk, ci=ci, c0=c0, wslot=wslot: e.matmul(
                        out=PS[bk + ci][:, 0:512],
                        lhsT=fr(wslot + kc * 128, 128),
                        rhs=fr(XZ0 + kc * TC + (c0 - HALO) + k + 1, 512),
                        start=(k == 0 and kc == 0), stop=(k == 3 and kc == NKC - 1))
                    if ci == 1 and kc == NKC - 1:
                        cvk_mm[(oc, k)] = P.op("tensor", TM, fn)
                    else:
                        P.op("tensor", None, fn)
            if k == 3:
                P.wait("scalar", TM, cvk_mm[(oc, 3)])
                P.wait("scalar", TM, ip_mm_v[31])
                for ci in range(2):
                    cv_evac_v[oc] = P.op("scalar", SA, lambda e, oc=oc, bk=bk, ci=ci: e.activation(
                        out=fr(XCB + oc * TL + ci * 512, 512), in_=PS[bk + ci][:, 0:512],
                        func=AF.Silu, bias=cbs[:, oc:oc + 1]))
                P.wait("sync", SA, cv_evac_v[oc])
                xc_dma_v[oc] = P.dma("sync", LD, xcst[oc * 128:(oc + 1) * 128, :],
                                     faf(XCB + oc * TL, TL))

        for j in range(len(cv_steps) + 2):
            if j >= 2:
                cv_step_compute(j - 2)
            if j < len(cv_steps):
                oc, k = cv_steps[j]
                if j < 2:
                    P.wait("gpsimd", TM, ip_mm_v[31])
                    P.wait("gpsimd", LD, z_dma_v[31])
                else:
                    P.wait("gpsimd", TM, cvk_mm[cv_steps[j - 2]])
                cv_load_v[(oc, k)] = P.dma("gpsimd", GP, fr(WIN[j % 2], 2048),
                                           wc[oc * 128:(oc + 1) * 128, k * 2048:(k + 1) * 2048])

        xpw = XPWr[:, 0:1536]
        P.wait("gpsimd", LD, z_dma_v[31])
        v_xpw = P.dma("gpsimd", GP, xpw, xpv)
        DBC = SQ[0]
        xp_mm_v = [None] * 2
        v_dbc = None
        P.wait("tensor", SA, cv_evac_v[15])
        P.wait("tensor", SA, ip_evac_v[31])
        P.wait("tensor", GP, v_xpw)
        for tci in range(2):
            for kc in range(NKC):
                fn = lambda e, kc=kc, tci=tci: e.matmul(
                    out=PS[4 + tci][0:96, 0:512],
                    lhsT=xpw[:, kc * 96:(kc + 1) * 96],
                    rhs=fr(XCB + kc * TL + tci * 512, 512),
                    start=(kc == 0), stop=(kc == NKC - 1))
                if kc == NKC - 1:
                    xp_mm_v[tci] = P.op("tensor", TM, fn)
                else:
                    P.op("tensor", None, fn)
            P.wait("vector", TM, xp_mm_v[tci])
            v_dbc = P.op("vector", SV, lambda e, tci=tci: e.tensor_copy(
                out=fa(DBC + tci * 512, 512, parts=96), in_=PS[4 + tci][0:96, 0:512]))
        P.wait("sync", SV, v_dbc)
        v_dbcst = P.dma("sync", LD, dbcst, faf(DBC, TL, parts=96))

        P.wait("gpsimd", LD, max(xc_dma_v[15], v_dbcst))
        v_cc_xc = P.cc("gpsimd", GP, "AllGather", xcst, xcg)
        v_cc_dbc = P.cc("gpsimd", GP, "AllGather", dbcst, dbcg)

        DEL2 = 0
        U2 = 4096
        XCC = 8192
        DPRE = 12288
        AB = 16384
        BT = 17408
        ST2 = 18432
        SCT = 19456
        BMB = 23552
        CMB = 24576
        YSC = 25600
        SPS = 26624
        BMS = 27648

        P.wait("gpsimd", GP, v_cc_dbc)
        v_dpre = None
        v_bms = None
        for r in range(4):
            v_dpre = P.dma("gpsimd", GP, fr(DPRE + r * TL, TL, parts=64),
                           dbcg[r * 96:r * 96 + 64, :])
            v_bms = P.dma("gpsimd", GP, fr(BMS + r * TL, TL, parts=32),
                          dbcg[r * 96 + 64:r * 96 + 96, :])
        P.wait("sync", GP, v_bms)
        v_bmcmd = P.dma("sync", LD, bmcmd, fa(BMS, L, parts=32))

        v_idx = None
        for r in range(4):
            for hb in range(4):
                j = r * 4 + hb
                P.op("vector", SV, lambda e, r=r, hb=hb: e.tensor_scalar(
                    out=idxsf, in0=iotaf, scalar1=qcolf[:, 0:1],
                    scalar2=float(r * 2048 + hb * 128), op0=ALU.add, op1=ALU.add))
                v_idx = P.op("vector", SV, lambda e, j=j: e.tensor_copy(
                    out=idxg[:, j:j + 1], in_=idxsf))
        v_idxy = None
        for kc in range(NKC):
            P.op("vector", SV, lambda e, kc=kc: e.tensor_scalar(
                out=idxsf, in0=iota4f, scalar1=qcolf[:, 1:2],
                scalar2=float(kc * 512), op0=ALU.add, op1=ALU.add))
            v_idxy = P.op("vector", SV, lambda e, kc=kc: e.tensor_copy(
                out=idxy[:, kc:kc + 1], in_=idxsf))
        P.wait("gpsimd", SV, v_idx)
        P.wait("gpsimd", GP, v_cc_xc)

        NIT = 128
        bm_ld_v = [None] * NIT
        xcc_v = {}
        u_v = {}
        a_v = {}
        tts_v = {}
        sc_v = {}
        mm_v = {}
        yev_v = {}
        ysd_v = {}
        dt_relu = {}

        def emit_tc_prep(tc):
            par = tc % 2
            r = tc // 2
            ybk = (tc % 2) * 4
            for hb in range(4):
                if tc >= 2:
                    P.wait("gpsimd", SV, yev_v[(tc - 2, hb)])
                xcc_v[(tc, hb)] = P.idma(
                    "gpsimd", GP, fr(XCC + par * 2048 + hb * 512, 512), xcg,
                    idxg[:, (r * 4 + hb):(r * 4 + hb) + 1],
                    element_offset=(tc % 2) * 512)
                bk = ybk + hb
                P.wait("tensor", GP, v_dpre)
                if tc >= 2:
                    P.wait("tensor", SV, yev_v[(tc - 2, hb)])
                v_dtmm = P.op("tensor", TM, lambda e, hb=hb, tc=tc, bk=bk: e.matmul(
                    out=PS[bk][:, 0:512], lhsT=dtws[:, hb * 128:(hb + 1) * 128],
                    rhs=fr(DPRE + tc * 512, 512, parts=64), start=True, stop=True))
                m = tc * 4 + hb
                P.wait("scalar", TM, v_dtmm)
                if m >= 2:
                    ptc, phb = divmod(m - 2, 4)
                    P.wait("scalar", SV, u_v[(ptc, phb)])
                sps = SPS + (m % 2) * 512
                dslot = DEL2 + par * 2048 + hb * 512
                P.op("scalar", SA, lambda e, bk=bk, sps=sps, hb=hb: e.activation(
                    out=fa(sps, 512), in_=PS[bk][:, 0:512], func=AF.Abs, bias=dtbs[:, hb:hb + 1]))
                P.op("scalar", SA, lambda e, sps=sps: e.activation(
                    out=fa(sps, 512), in_=fa(sps, 512), func=AF.Exp, scale=-1.0))
                P.op("scalar", SA, lambda e, sps=sps: e.activation(
                    out=fa(sps, 512), in_=fa(sps, 512), func=AF.Ln, bias=1.0))
                v_relu = P.op("scalar", SA, lambda e, bk=bk, hb=hb, dslot=dslot: e.activation(
                    out=fa(dslot, 512), in_=PS[bk][:, 0:512], func=AF.Relu, bias=dtbs[:, hb:hb + 1]))
                dt_relu[(tc, hb)] = v_relu
                if tc >= 2:
                    P.wait("vector", SA, a_v[(tc - 2, 15, hb)])
                P.wait("vector", SA, v_relu)
                P.op("vector", SV, lambda e, dslot=dslot, sps=sps: e.tensor_add(
                    out=fa(dslot, 512), in0=fa(dslot, 512), in1=fa(sps, 512)))
                P.wait("vector", GP, xcc_v[(tc, hb)])
                u_v[(tc, hb)] = P.op("vector", SV, lambda e, dslot=dslot, par=par, hb=hb: e.tensor_mul(
                    out=fa(U2 + par * 2048 + hb * 512, 512), in0=fa(dslot, 512),
                    in1=fa(XCC + par * 2048 + hb * 512, 512)))

        def emit_bm_load(i):
            tc, n = divmod(i, 16)
            par = i % 2
            if i >= 2:
                P.wait("sync", SV, sc_v[(i - 2, 3)])
            P.wait("sync", LD, v_bmcmd)
            P.dma("sync", LD, fa(BMB + par * 512, 512),
                  bmcmd[n:n + 1, tc * 512:(tc + 1) * 512].to_broadcast((128, 512)))
            bm_ld_v[i] = P.dma("sync", LD, fa(CMB + par * 512, 512),
                               bmcmd[16 + n:17 + n, tc * 512:(tc + 1) * 512].to_broadcast((128, 512)))

        def emit_scan_iter(i):
            tc, n = divmod(i, 16)
            par = i % 2
            tpar = tc % 2
            ybk = (tc % 2) * 4
            for hb in range(4):
                cidx = hb * 16 + n
                g = i * 4 + hb
                aslot = (g % 2) * 512
                dslot = DEL2 + tpar * 2048 + hb * 512
                if g >= 2:
                    pi, phb = divmod(g - 2, 4)
                    P.wait("scalar", SV, tts_v[(pi, phb)])
                P.wait("scalar", SV, u_v[(tc, hb)])
                P.wait("scalar", SV, v_aneg)
                a_v[(tc, n, hb)] = P.op("scalar", SA, lambda e, hb=hb, n=n, aslot=aslot, dslot=dslot:
                                        e.activation(out=fo(ABf, aslot, 512), in_=fa(dslot, 512),
                                                     func=AF.Exp,
                                                     scale=aneg[:, hb * 16 + n:hb * 16 + n + 1]))
                P.wait("vector", LD, bm_ld_v[i])
                P.op("vector", SV, lambda e, hb=hb, tpar=tpar, par=par: e.tensor_mul(
                    out=fo(BTf, par * 512, 512), in0=fa(U2 + tpar * 2048 + hb * 512, 512),
                    in1=fa(BMB + par * 512, 512)))
                P.wait("vector", SA, a_v[(tc, n, hb)])
                tts_v[(i, hb)] = P.op("vector", SV, lambda e, par=par, aslot=aslot, cidx=cidx:
                                      e.tensor_tensor_scan(out=fo(STf, par * 512, 512),
                                                           data0=fo(ABf, aslot, 512),
                                                           data1=fo(BTf, par * 512, 512),
                                                           initial=carry[:, cidx:cidx + 1],
                                                           op0=ALU.mult, op1=ALU.add))
                P.op("vector", SV, lambda e, par=par, cidx=cidx: e.tensor_copy(
                    out=carry[:, cidx:cidx + 1], in_=fo(STf, par * 512 + 511, 1)))
                scs = SCT + (hb * 2 + par) * 512
                if i >= 2:
                    P.wait("vector", TM, mm_v[(i - 2, hb)])
                sc_v[(i, hb)] = P.op("vector", SV, lambda e, par=par, scs=scs: e.tensor_mul(
                    out=fr(scs, 512), in0=fo(STf, par * 512, 512),
                    in1=fa(CMB + par * 512, 512)))
                P.wait("tensor", SV, sc_v[(i, hb)])
                mm_v[(i, hb)] = P.op("tensor", TM, lambda e, scs=scs, ybk=ybk, hb=hb, n=n: e.matmul(
                    out=PS[ybk + hb][:, 0:512], lhsT=idents, rhs=fr(scs, 512),
                    start=(n == 0), stop=(n == 15)))

        def emit_y_evac(tc):
            ybk = (tc % 2) * 4
            tpar = tc % 2
            for hb in range(4):
                P.wait("vector", TM, mm_v[(tc * 16 + 15, hb)])
                j = tc * 4 + hb
                ysl = YSC + (j % 2) * 512
                if j >= 2:
                    P.wait("vector", LD, ysd_v[j - 2])
                P.op("vector", SV, lambda e, hb=hb, tpar=tpar: e.tensor_scalar(
                    out=dxs[:], in0=fa(XCC + tpar * 2048 + hb * 512, 512),
                    scalar1=dvs[:, hb:hb + 1], scalar2=None, op0=ALU.mult))
                yev_v[(tc, hb)] = P.op("vector", SV, lambda e, ybk=ybk, hb=hb, ysl=ysl: e.tensor_add(
                    out=fa(ysl, 512), in0=PS[ybk + hb][:, 0:512], in1=dxs[:]))
                P.wait("sync", SV, yev_v[(tc, hb)])
                ysd_v[j] = P.dma("sync", LD, yst[hb * 128:(hb + 1) * 128, tc * 512:(tc + 1) * 512],
                                 faf(ysl, 512))

        emit_tc_prep(0)
        emit_bm_load(0)
        for i in range(NIT):
            tc, n = divmod(i, 16)
            if n == 0 and tc + 1 < 8:
                emit_tc_prep(tc + 1)
            if i + 1 < NIT:
                emit_bm_load(i + 1)
            emit_scan_iter(i)
            if n == 15:
                emit_y_evac(tc)

        P.wait("gpsimd", LD, ysd_v[31])
        v_cc_y = P.cc("gpsimd", GP, "AllGather", yst, yg)

        XR = 0
        V3 = 16384
        RB1 = 32768
        MRB = 33792
        T1 = 34816
        WOS = (35840, 37888)
        ZYT = (39936, 40960)
        OUTS = (39936, 40960)
        SQ3 = WOS[0]
        G3B = 0
        XR2B = 16384
        W1EB = 34816

        vv_v = [None] * NKC
        sz_done = [None] * NKC
        P.wait("sync", LD, ysd_v[31])
        P.wait("vector", LD, bm_ld_v[NIT - 1])
        P.wait("gpsimd", GP, v_cc_y)
        P.wait("gpsimd", SV, v_idxy)
        for kc in range(NKC):
            zslot, yslot = ZYT
            if kc >= 1:
                P.wait("sync", SA, sz_done[kc - 1])
                P.wait("sync", SV, vv_v[kc - 1])
            v_z = P.dma("sync", LD, fa(zslot, TL), zdram[kc * 128:(kc + 1) * 128, :])
            if kc >= 1:
                P.wait("gpsimd", SV, vv_v[kc - 1])
            v_y = P.idma("gpsimd", GP, fa(yslot, TL), ygv, idxy[:, kc:kc + 1])
            P.wait("scalar", LD, v_z)
            sz_done[kc] = P.op("scalar", SA, lambda e, zslot=zslot: e.activation(
                out=fa(zslot, TL), in_=fa(zslot, TL), func=AF.Silu))
            P.wait("vector", SA, sz_done[kc])
            P.wait("vector", GP, v_y)
            vv_v[kc] = P.op("vector", SV, lambda e, kc=kc, zslot=zslot, yslot=yslot: e.tensor_mul(
                out=fr(V3 + kc * TL, TL), in0=fa(yslot, TL), in1=fa(zslot, TL)))

        xr_v = [P.dma("sync", LD, fa(XR + kc * TL, TL), x_fm[kc * 128:(kc + 1) * 128, HALO:TC])
                for kc in range(NKC)]

        op_load_v = [None] * NKC
        op_mm_v = [None] * NKC
        op_evac_v = [None] * NKC

        def op_compute(oc):
            bk = 0 if oc % 2 == 0 else 2
            P.wait("tensor", GP, op_load_v[oc])
            P.wait("tensor", SV, vv_v[NKC - 1])
            if oc >= 2:
                P.wait("tensor", SV, op_evac_v[oc - 2])
            for tci in range(2):
                for kc in range(NKC):
                    fn = lambda e, oc=oc, tci=tci, kc=kc, bk=bk: e.matmul(
                        out=PS[bk + tci][:, 0:512],
                        lhsT=fr(WOS[oc % 2] + kc * 128, 128),
                        rhs=fr(V3 + kc * TL + tci * 512, 512),
                        start=(kc == 0), stop=(kc == NKC - 1))
                    if tci == 1 and kc == NKC - 1:
                        op_mm_v[oc] = P.op("tensor", TM, fn)
                    else:
                        P.op("tensor", None, fn)
            P.wait("vector", TM, op_mm_v[oc])
            P.wait("vector", LD, xr_v[oc])
            for tci in range(2):
                op_evac_v[oc] = P.op("vector", SV, lambda e, oc=oc, tci=tci, bk=bk:
                                     e.scalar_tensor_tensor(
                                         out=fr(XR + oc * TL + tci * 512, 512),
                                         in0=fa(XR + oc * TL + tci * 512, 512), scalar=2.0,
                                         in1=PS[bk + tci][:, 0:512],
                                         op0=ALU.mult, op1=ALU.add))

        for i in range(18):
            if i >= 2:
                op_compute(i - 2)
            if i < 16:
                if i >= 2:
                    P.wait("gpsimd", TM, op_mm_v[i - 2])
                else:
                    P.wait("gpsimd", SV, sc_v[(NIT - 1, 3)])
                op_load_v[i] = P.dma("gpsimd", GP, fr(WOS[i % 2], 2048), wo[i * 128:(i + 1) * 128, :])

        mmq = {}
        v_mr = None
        v_rowread = None
        for tci in range(2):
            if v_rowread is not None:
                P.wait("tensor", SV, v_rowread)
            for kc in range(NKC):
                P.wait("tensor", SV, op_evac_v[kc])
                fn = lambda e, kc=kc, tci=tci: e.matmul(
                    out=PS[6][0:1, 0:512], lhsT=onest, rhs=fr(XR + kc * TL + tci * 512, 512),
                    start=(kc == 0), stop=(kc == NKC - 1))
                if kc == NKC - 1:
                    v_sum = P.op("tensor", TM, fn)
                else:
                    P.op("tensor", None, fn)
            for kc in range(NKC):
                P.wait("scalar", SV, op_evac_v[kc])
                P.wait("scalar", TM, op_mm_v[NKC - 1])
                P.wait("scalar", GP, op_load_v[NKC - 1])
                if kc >= 2:
                    P.wait("scalar", TM, mmq[(tci, kc - 2)])
                elif tci == 1:
                    P.wait("scalar", TM, mmq[(0, NKC - 2 + kc)])
                sq3s = SQ3 + (kc % 2) * 512
                v_sq = P.op("scalar", SA, lambda e, kc=kc, tci=tci, sq3s=sq3s: e.activation(
                    out=fr(sq3s, 512), in_=fa(XR + kc * TL + tci * 512, 512), func=AF.Square))
                P.wait("tensor", SA, v_sq)
                fn = lambda e, kc=kc, sq3s=sq3s: e.matmul(
                    out=PS[7][0:1, 0:512], lhsT=onest, rhs=fr(sq3s, 512),
                    start=(kc == 0), stop=(kc == NKC - 1))
                mmq[(tci, kc)] = P.op("tensor", TM, fn)
            P.wait("vector", TM, mmq[(tci, NKC - 1)])
            P.wait("vector", TM, v_sum)
            P.op("vector", SV, lambda e, tci=tci: e.tensor_scalar(
                out=fo(MRBf, tci * 512, 512, parts=1), in0=PS[6][0:1, 0:512],
                scalar1=1.0 / H, scalar2=None, op0=ALU.mult))
            v_rowread = P.op("vector", SV, lambda e, tci=tci: e.tensor_scalar(
                out=fo(RB1f, tci * 512, 512, parts=1), in0=PS[7][0:1, 0:512],
                scalar1=1.0 / H, scalar2=None, op0=ALU.mult))
            P.op("vector", SV, lambda e, tci=tci: e.tensor_mul(
                out=fo(T1f, tci * 512, 512, parts=1), in0=fo(MRBf, tci * 512, 512, parts=1),
                in1=fo(MRBf, tci * 512, 512, parts=1)))
            v_var = P.op("vector", SV, lambda e, tci=tci: e.tensor_sub(
                out=fo(RB1f, tci * 512, 512, parts=1), in0=fo(RB1f, tci * 512, 512, parts=1),
                in1=fo(T1f, tci * 512, 512, parts=1)))
            P.wait("scalar", SV, v_var)
            P.op("scalar", SA, lambda e, tci=tci: e.activation(
                out=fo(RB1f, tci * 512, 512, parts=1), in_=fo(RB1f, tci * 512, 512, parts=1),
                func=AF.Ln, bias=cB[0:1, 0:1]))
            v_rs = P.op("scalar", SA, lambda e, tci=tci: e.activation(
                out=fo(RB1f, tci * 512, 512, parts=1), in_=fo(RB1f, tci * 512, 512, parts=1),
                func=AF.Exp, scale=-0.5))
            P.wait("vector", SA, v_rs)
            v_mr = P.op("vector", SV, lambda e, tci=tci: e.tensor_mul(
                out=fo(MRBf, tci * 512, 512, parts=1), in0=fo(MRBf, tci * 512, 512, parts=1),
                in1=fo(RB1f, tci * 512, 512, parts=1)))
        v_smr = P.op("vector", SV, lambda e: e.tensor_reduce(
            out=fo(T1f, 0, 1, parts=1), in_=fo(MRBf, 0, TL, parts=1), axis=mybir.AxisListType.X, op=ALU.add))
        P.wait("sync", SV, v_smr)
        P.dma("sync", LD, rowb[0:1, 0:TL], fo(RB1f, 0, TL, parts=1))
        P.dma("sync", LD, rowb[0:1, TL:2 * TL], fo(MRBf, 0, TL, parts=1))
        v_w = P.dma("sync", LD, rowb[0:1, 2 * TL:2 * TL + 1], fo(T1f, 0, 1, parts=1))
        P.wait("sync", LD, v_w)
        P.dma("sync", LD, fo(RB1f, 0, TL), rowb[0:1, 0:TL].to_broadcast((128, TL)))
        P.dma("sync", LD, smrB, rowb[0:1, 2 * TL:2 * TL + 1].to_broadcast((128, 1)))
        v_bc = P.dma("sync", LD, fo(MRBf, 0, TL), rowb[0:1, TL:2 * TL].to_broadcast((128, TL)))
        P.wait("vector", LD, v_bc)
        xln_v = [None] * NKC
        for kc in range(NKC):
            P.op("vector", SV, lambda e, kc=kc: e.tensor_mul(
                out=fo(T1f, 0, TL), in0=fa(XR + kc * TL, TL), in1=fo(RB1f, 0, TL)))
            P.op("vector", SV, lambda e, kc=kc: e.tensor_reduce(
                out=gsum[:, kc:kc + 1], in_=fo(T1f, 0, TL), axis=mybir.AxisListType.X, op=ALU.add))
            xln_v[kc] = P.op("vector", SV, lambda e, kc=kc: e.tensor_sub(
                out=ba(kc * TL, TL), in0=fo(T1f, 0, TL), in1=fo(MRBf, 0, TL)))
        v_gl = P.op("vector", SV, lambda e: e.tensor_scalar(
            out=gsum, in0=gsum, scalar1=smrB[:, 0:1], scalar2=None, op0=ALU.subtract))
        P.wait("sync", SV, v_gl)
        v_gst = P.dma("sync", LD, gst, gsum)
        P.wait("gpsimd", LD, v_gst)
        v_cc_g = P.cc("gpsimd", GP, "AllGather", gst, gg)
        P.wait("sync", GP, v_cc_g)
        v_ggr = None
        for r in range(4):
            v_ggr = P.dma("sync", LD, ggr[:, r * NKC:(r + 1) * NKC], gg[r * 128:(r + 1) * 128, :])
        P.wait("vector", LD, v_ggr)
        P.op("vector", SV, lambda e: e.tensor_add(out=m128, in0=ggr[:, 0:NKC], in1=ggr[:, NKC:2 * NKC]))
        P.op("vector", SV, lambda e: e.tensor_add(out=m128, in0=m128, in1=ggr[:, 2 * NKC:3 * NKC]))
        P.op("vector", SV, lambda e: e.tensor_add(out=m128, in0=m128, in1=ggr[:, 3 * NKC:4 * NKC]))
        v_m128 = P.op("vector", SV, lambda e: e.tensor_scalar(
            out=m128r, in0=m128, scalar1=1.0 / L, scalar2=None, op0=ALU.mult))
        P.wait("tensor", SV, v_m128)
        for kc in range(NKC):
            fn = lambda e, kc=kc: e.matmul(
                out=PS[4][0:1, 0:6], lhsT=m128r[:, kc:kc + 1], rhs=gws[:, kc * 6:(kc + 1) * 6],
                start=(kc == 0), stop=(kc == NKC - 1))
            if kc == NKC - 1:
                v_glog = P.op("tensor", TM, fn)
            else:
                P.op("tensor", None, fn)
        P.wait("vector", TM, v_glog)
        P.op("vector", SV, lambda e: e.memset(ls8, -3.0e38))
        P.op("vector", SV, lambda e: e.tensor_copy(out=ls8[:, 0:6], in_=PS[4][0:1, 0:6]))
        P.op("vector", SV, lambda e: e.max(out=mx8, in_=ls8))
        P.op("vector", SV, lambda e: e.tensor_scalar(
            out=eq8, in0=ls8, scalar1=mx8[:, 0:1], scalar2=None, op0=ALU.is_equal))
        P.op("vector", SV, lambda e: e.tensor_mul(out=eq8, in0=eq8, in1=iota8s))
        P.op("vector", SV, lambda e: e.tensor_reduce(
            out=grow[:, 2:3], in_=eq8, axis=mybir.AxisListType.X, op=ALU.add))
        P.op("vector", SV, lambda e: e.tensor_scalar(
            out=eq8, in0=ls8, scalar1=mx8[:, 1:2], scalar2=None, op0=ALU.is_equal))
        P.op("vector", SV, lambda e: e.tensor_mul(out=eq8, in0=eq8, in1=iota8s))
        P.op("vector", SV, lambda e: e.tensor_reduce(
            out=grow[:, 3:4], in_=eq8, axis=mybir.AxisListType.X, op=ALU.add))
        v_d01 = P.op("vector", SV, lambda e: e.tensor_sub(out=gtmp[:, 0:1], in0=mx8[:, 0:1], in1=mx8[:, 1:2]))
        P.wait("scalar", SV, v_d01)
        v_w0 = P.op("scalar", SA, lambda e: e.activation(out=grow[:, 0:1], in_=gtmp[:, 0:1], func=AF.Sigmoid))
        P.wait("vector", SA, v_w0)
        v_grow = P.op("vector", SV, lambda e: e.tensor_scalar(
            out=grow[:, 1:2], in0=grow[:, 0:1], scalar1=-1.0, scalar2=1.0, op0=ALU.mult, op1=ALU.add))
        P.wait("sync", SV, v_grow)
        v_w = P.dma("sync", LD, wrow, grow)
        P.wait("sync", LD, v_w)
        v_wB = P.dma("sync", LD, wB, wrow.to_broadcast((128, 8)))
        P.wait("vector", LD, v_wB)
        v_ecol = None
        for j, (src, mul) in enumerate(((2, 2048.0), (2, 1408.0), (2, 128.0),
                                        (3, 2048.0), (3, 1408.0), (3, 128.0))):
            v_ecol = P.op("vector", SV, lambda e, j=j, src=src, mul=mul: e.tensor_scalar(
                out=ecol[:, j:j + 1], in0=wB[:, src:src + 1], scalar1=mul, scalar2=None, op0=ALU.mult))

        _bg_prev = [None]

        def bias_gather(col, table, dst_ap):
            if _bg_prev[0] is not None:
                P.wait("vector", GP, _bg_prev[0])
            P.op("vector", SV, lambda e, col=col: e.tensor_add(
                out=idxsf, in0=iotaf, in1=ecol[:, col:col + 1]))
            v2 = P.op("vector", SV, lambda e: e.tensor_copy(out=idxtb, in_=idxsf))
            P.wait("gpsimd", SV, v2)
            _bg_prev[0] = P.idma("gpsimd", GP, dst_ap, table, idxtb)
            return _bg_prev[0]

        P.wait("vector", SV, v_ecol)
        bias_gather(2, b1fm, b1g[:, 0:NFC])
        bias_gather(5, b1fm, b1g[:, NFC:2 * NFC])
        bias_gather(2, b2fm, b2g[:, 0:NKC])
        v_bg = bias_gather(5, b2fm, b2g[:, NKC:2 * NKC])
        P.wait("vector", GP, v_bg)
        P.op("vector", SV, lambda e: e.tensor_scalar(
            out=b2c, in0=b2g[:, 0:NKC], scalar1=wB[:, 0:1], scalar2=None, op0=ALU.mult))
        v_b2c = P.op("vector", SV, lambda e: e.scalar_tensor_tensor(
            out=b2c, in0=b2g[:, NKC:2 * NKC], scalar=wB[:, 1:2], in1=b2c,
            op0=ALU.mult, op1=ALU.add))

        moe_gv = []

        def moe_idx(col, addi):
            P.op("vector", SV, lambda e, col=col: e.tensor_add(
                out=idxsf, in0=iotaf, in1=ecol[:, col:col + 1]))
            if addi:
                P.op("vector", SV, lambda e, addi=addi: e.tensor_scalar(
                    out=idxsf, in0=idxsf, scalar1=float(addi), scalar2=None, op0=ALU.add))
            slot = moe_idx.c % 16
            if moe_idx.c >= 16:
                P.wait("vector", GP, moe_gv[moe_idx.c - 16])
            moe_idx.c += 1
            v = P.op("vector", SV, lambda e, slot=slot: e.tensor_copy(
                out=idxm[:, slot:slot + 1], in_=idxsf))
            return v, idxm[:, slot:slot + 1]
        moe_idx.c = 0

        ff_evac = {}
        gel_all = {}
        gel_scaled_v = {}
        h1_last_mm = [None, None]
        ff_last_mm = [None, None]

        for s in range(2):
            ec0 = 0 if s == 0 else 3
            v_w1g = None
            for kc in range(NKC):
                vi, iap = moe_idx(ec0 + 0, kc * 128)
                P.wait("gpsimd", SV, vi)
                if s == 0:
                    P.wait("gpsimd", TM, op_mm_v[NKC - 1])
                else:
                    P.wait("gpsimd", TM, ff_last_mm[0])
                v_w1g = P.idma("gpsimd", GP, fo(W1B, kc * DFF, DFF), w1f, iap)
                moe_gv.append(v_w1g)
            P.wait("tensor", GP, v_w1g)
            P.wait("tensor", SV, xln_v[NKC - 1])
            if s == 0:
                P.wait("tensor", SV, op_evac_v[NKC - 1])
            for fc in range(NFC):
                for tci in range(2):
                    g = fc * 2 + tci
                    bk = g % 4
                    if g >= 4:
                        P.wait("tensor", SA, gel_all[(s, g - 4)])
                    elif s == 1:
                        P.wait("tensor", SA, gel_all[(0, NFC * 2 - 4 + g)])
                    for kc in range(NKC):
                        fn = lambda e, fc=fc, tci=tci, kc=kc, bk=bk: e.matmul(
                            out=PS[bk][:, 0:512],
                            lhsT=fo(W1B, kc * DFF + fc * 128, 128),
                            rhs=ba(kc * TL + tci * 512, 512),
                            start=(kc == 0), stop=(kc == NKC - 1))
                        if kc == NKC - 1:
                            v_h1 = P.op("tensor", TM, fn)
                        else:
                            P.op("tensor", None, fn)
                    if fc == NFC - 1 and tci == 1:
                        h1_last_mm[s] = v_h1
                    P.wait("scalar", TM, v_h1)
                    if s == 1 and g < 4:
                        P.wait("scalar", TM, ff_last_mm[0])
                    gel_all[(s, g)] = P.op("scalar", SA, lambda e, fc=fc, tci=tci, bk=bk, s=s: e.activation(
                        out=fo(GB, fc * TL + tci * 512, 512), in_=PS[bk][:, 0:512],
                        func=AF.Gelu, bias=b1g[:, s * NFC + fc:s * NFC + fc + 1]))
                    P.wait("vector", SA, gel_all[(s, g)])
                    gel_scaled_v[(s, g)] = P.op("vector", SV, lambda e, fc=fc, tci=tci, s=s: e.tensor_scalar(
                        out=fo(GB, fc * TL + tci * 512, 512), in0=fo(GB, fc * TL + tci * 512, 512),
                        scalar1=wB[:, s:s + 1], scalar2=None, op0=ALU.mult))
            v_w2g = None
            for fc in range(NFC):
                vi, iap = moe_idx(ec0 + 1, fc * 128)
                P.wait("gpsimd", SV, vi)
                P.wait("gpsimd", TM, h1_last_mm[s])
                v_w2g = P.idma("gpsimd", GP, fo(W1B, fc * H, H), w2f, iap)
                moe_gv.append(v_w2g)
            P.wait("tensor", GP, v_w2g)
            P.wait("vector", SV, v_b2c)
            for oc in range(NKC):
                for tci in range(2):
                    g = oc * 2 + tci
                    bk = 4 + g % 4
                    if g >= 4:
                        P.wait("tensor", SV, ff_evac[(s, g - 4)])
                    elif s == 1:
                        P.wait("tensor", SV, ff_evac[(0, NKC * 2 - 4 + g)])
                    for fc in range(NFC):
                        P.wait("tensor", SV, gel_scaled_v[(s, fc * 2 + tci)])
                        fn = lambda e, oc=oc, tci=tci, fc=fc, bk=bk: e.matmul(
                            out=PS[bk][:, 0:512],
                            lhsT=fo(W1B, fc * H + oc * 128, 128),
                            rhs=fo(GB, fc * TL + tci * 512, 512),
                            start=(fc == 0), stop=(fc == NFC - 1))
                        if fc == NFC - 1:
                            v_ff = P.op("tensor", TM, fn)
                        else:
                            P.op("tensor", None, fn)
                    if oc == NKC - 1 and tci == 1:
                        ff_last_mm[s] = v_ff
                    P.wait("vector", TM, v_ff)
                    if s == 0:
                        ff_evac[(s, g)] = P.op("vector", SV, lambda e, oc=oc, tci=tci, bk=bk:
                                               e.scalar_tensor_tensor(
                                                   out=fo(X2B, oc * TL + tci * 512, 512),
                                                   in0=PS[bk][:, 0:512], scalar=b2c[:, oc:oc + 1],
                                                   in1=ba(oc * TL + tci * 512, 512),
                                                   op0=ALU.add, op1=ALU.add))
                    else:
                        ff_evac[(s, g)] = P.op("vector", SV, lambda e, oc=oc, tci=tci, bk=bk:
                                               e.tensor_add(
                                                   out=fo(X2B, oc * TL + tci * 512, 512),
                                                   in0=PS[bk][:, 0:512],
                                                   in1=fo(X2B, oc * TL + tci * 512, 512)))

        P.wait("tensor", SV, ff_evac[(1, NKC * 2 - 1)])
        mmq2 = {}
        v_mr2 = None
        v_rowread2 = None
        for tci in range(2):
            if v_rowread2 is not None:
                P.wait("tensor", SV, v_rowread2)
            for kc in range(NKC):
                fn = lambda e, kc=kc, tci=tci: e.matmul(
                    out=PS[0][0:1, 0:512], lhsT=onesb[:], rhs=fo(X2B, kc * TL + tci * 512, 512),
                    start=(kc == 0), stop=(kc == NKC - 1))
                if kc == NKC - 1:
                    v_sum2 = P.op("tensor", TM, fn)
                else:
                    P.op("tensor", None, fn)
            for kc in range(NKC):
                if kc < 2:
                    P.wait("scalar", TM, v_sum2)
                    if tci == 1:
                        P.wait("scalar", TM, mmq2[(0, NKC - 2 + kc)])
                else:
                    P.wait("scalar", TM, mmq2[(tci, kc - 2)])
                bas = (kc % 2) * 512
                v_sq2 = P.op("scalar", SA, lambda e, kc=kc, tci=tci, bas=bas: e.activation(
                    out=ba(bas, 512), in_=fo(X2B, kc * TL + tci * 512, 512), func=AF.Square))
                P.wait("tensor", SA, v_sq2)
                fn = lambda e, kc=kc, bas=bas: e.matmul(
                    out=PS[1][0:1, 0:512], lhsT=onesb[:], rhs=ba(bas, 512),
                    start=(kc == 0), stop=(kc == NKC - 1))
                mmq2[(tci, kc)] = P.op("tensor", TM, fn)
            P.wait("vector", TM, mmq2[(tci, NKC - 1)])
            P.wait("vector", TM, v_sum2)
            P.op("vector", SV, lambda e, tci=tci: e.tensor_scalar(
                out=fo(MRBf, tci * 512, 512, parts=1), in0=PS[0][0:1, 0:512],
                scalar1=1.0 / H, scalar2=None, op0=ALU.mult))
            v_rowread2 = P.op("vector", SV, lambda e, tci=tci: e.tensor_scalar(
                out=fo(RB1f, tci * 512, 512, parts=1), in0=PS[1][0:1, 0:512],
                scalar1=1.0 / H, scalar2=None, op0=ALU.mult))
            P.op("vector", SV, lambda e, tci=tci: e.tensor_mul(
                out=fo(T1f, tci * 512, 512, parts=1), in0=fo(MRBf, tci * 512, 512, parts=1),
                in1=fo(MRBf, tci * 512, 512, parts=1)))
            v_var2 = P.op("vector", SV, lambda e, tci=tci: e.tensor_sub(
                out=fo(RB1f, tci * 512, 512, parts=1), in0=fo(RB1f, tci * 512, 512, parts=1),
                in1=fo(T1f, tci * 512, 512, parts=1)))
            P.wait("scalar", SV, v_var2)
            P.op("scalar", SA, lambda e, tci=tci: e.activation(
                out=fo(RB1f, tci * 512, 512, parts=1), in_=fo(RB1f, tci * 512, 512, parts=1),
                func=AF.Ln, bias=cB[0:1, 0:1]))
            v_rs2 = P.op("scalar", SA, lambda e, tci=tci: e.activation(
                out=fo(RB1f, tci * 512, 512, parts=1), in_=fo(RB1f, tci * 512, 512, parts=1),
                func=AF.Exp, scale=-0.5))
            P.wait("vector", SA, v_rs2)
            v_mr2 = P.op("vector", SV, lambda e, tci=tci: e.tensor_mul(
                out=fo(MRBf, tci * 512, 512, parts=1), in0=fo(MRBf, tci * 512, 512, parts=1),
                in1=fo(RB1f, tci * 512, 512, parts=1)))
        P.wait("sync", SV, v_mr2)
        P.dma("sync", LD, rowb[0:1, 0:TL], fo(RB1f, 0, TL, parts=1))
        v_w = P.dma("sync", LD, rowb[0:1, TL:2 * TL], fo(MRBf, 0, TL, parts=1))
        P.wait("sync", LD, v_w)
        P.dma("sync", LD, fo(RB1f, 0, TL), rowb[0:1, 0:TL].to_broadcast((128, TL)))
        v_bc2 = P.dma("sync", LD, fo(MRBf, 0, TL), rowb[0:1, TL:2 * TL].to_broadcast((128, TL)))
        P.wait("vector", LD, v_bc2)
        outd = {}
        for kc in range(NKC):
            P.op("vector", SV, lambda e, kc=kc: e.tensor_mul(
                out=fo(T1f, 0, TL), in0=fo(X2B, kc * TL, TL), in1=fo(RB1f, 0, TL)))
            tgt = OUTS[0] if kc % 2 == 0 else OUTS[0] + 512
            tgt = ZYT[kc % 2]
            if kc >= 2:
                P.wait("vector", LD, outd[kc - 2])
            v_ap = P.op("vector", SV, lambda e, kc=kc, tgt=tgt: e.tensor_sub(
                out=fa(tgt, TL), in0=fo(T1f, 0, TL), in1=fo(MRBf, 0, TL)))
            P.wait("sync", SV, v_ap)
            outd[kc] = P.dma("sync", LD, out_fm[kc * 128:(kc + 1) * 128, :], fa(tgt, TL))

        P.finish()

        @block.sync
        def _(e):
            for f in P.q["sync"]:
                f(e)

        @block.gpsimd
        def _(e):
            for f in P.q["gpsimd"]:
                f(e)

        @block.scalar
        def _(e):
            for f in P.q["scalar"]:
                f(e)

        @block.vector
        def _(e):
            for f in P.q["vector"]:
                f(e)

        @block.tensor
        def _(e):
            for f in P.q["tensor"]:
                f(e)

    return nc


_NC_CACHE = None


def kernel(L0_attn_x, L0_attn_norm_w, L0_attn_in_proj_w, L0_attn_conv_w, L0_attn_conv_b,
           L0_attn_x_proj_w, L0_attn_dt_proj_w, L0_attn_dt_proj_b, L0_attn_A_log, L0_attn_D,
           L0_attn_out_proj_w, L0_ffn_gate_W, L0_ffn_W1, L0_ffn_b1, L0_ffn_W2, L0_ffn_b2):
    global _NC_CACHE
    f32 = np.float32
    x = np.asarray(L0_attn_x, f32)
    in_proj_w = np.asarray(L0_attn_in_proj_w, f32)
    conv_w = np.asarray(L0_attn_conv_w, f32)
    out_proj_w = np.asarray(L0_attn_out_proj_w, f32)
    x_proj_w = np.asarray(L0_attn_x_proj_w, f32)
    dt_proj_w = np.asarray(L0_attn_dt_proj_w, f32)
    dt_proj_b = np.asarray(L0_attn_dt_proj_b, f32)
    A_log = np.asarray(L0_attn_A_log, f32)
    D = np.asarray(L0_attn_D, f32)
    norm_w = np.asarray(L0_attn_norm_w, f32)
    conv_b = np.asarray(L0_attn_conv_b, f32)
    gate_W = np.asarray(L0_ffn_gate_W, f32)
    W1 = np.asarray(L0_ffn_W1, f32)
    b1 = np.asarray(L0_ffn_b1, f32)
    W2 = np.asarray(L0_ffn_W2, f32)
    b2 = np.asarray(L0_ffn_b2, f32)

    shared = dict(
        wi=np.ascontiguousarray(in_proj_w.reshape(32, 128, NKC, 128).transpose(0, 3, 2, 1)
                                .reshape(32 * 128, NKC * 128)),
        wc=np.ascontiguousarray(conv_w.reshape(4, NKC, 128, 16, 128).transpose(3, 2, 0, 1, 4)
                                .reshape(16 * 128, 4 * NKC * 128)),
        wo=np.ascontiguousarray(out_proj_w.reshape(16, 128, NKC, 128).transpose(0, 3, 2, 1)
                                .reshape(16 * 128, NKC * 128)),
        xpv=np.ascontiguousarray(x_proj_w.reshape(96, NKC, 128).transpose(2, 1, 0)
                                 .reshape(128, NKC * 96)),
        nw=np.ascontiguousarray(norm_w.reshape(NKC, 128).T),
        cbv=np.ascontiguousarray(conv_b.reshape(NKC, 128).T),
        gwv=np.ascontiguousarray(gate_W.reshape(6, NKC, 128).transpose(2, 1, 0)
                                 .reshape(128, NKC * 6)),
        w1f=np.ascontiguousarray(W1.reshape(6 * H, DFF)),
        w2f=np.ascontiguousarray(W2.reshape(6 * DFF, H)),
        b1fm=np.ascontiguousarray(b1.reshape(6, NFC, 128).transpose(0, 2, 1).reshape(6 * 128, NFC)),
        b2fm=np.ascontiguousarray(b2.reshape(6, NKC, 128).transpose(0, 2, 1).reshape(6 * 128, NKC)),
        identi=np.eye(128, dtype=f32),
        onesf=np.ones((128, 1), f32),
        iota8=np.arange(8, dtype=f32).reshape(1, 8),
        cvals=np.array([[EPS, 0.0]], f32),
    )

    in_maps = []
    for c in range(8):
        b, q = divmod(c, 4)
        t0 = q * TL
        xs = np.zeros((H, TC), f32)
        lo = max(0, t0 - HALO)
        xs[:, HALO - (t0 - lo):] = x[b, lo:t0 + TL, :].T
        m = dict(shared)
        m["x_fm"] = np.ascontiguousarray(xs)
        m["dtw"] = np.ascontiguousarray(dt_proj_w[q * 512:(q + 1) * 512, :].T)
        m["dtb"] = np.ascontiguousarray(dt_proj_b[q * 512:(q + 1) * 512].reshape(4, 128).T)
        m["alog"] = np.ascontiguousarray(A_log[q * 512:(q + 1) * 512].reshape(4, 128, NST)
                                         .transpose(1, 0, 2).reshape(128, 4 * NST))
        m["dvec"] = np.ascontiguousarray(D[q * 512:(q + 1) * 512].reshape(4, 128).T)
        m["qoff"] = np.array([[q * 512, q]], np.int32)
        in_maps.append(m)

    if _NC_CACHE is None:
        _NC_CACHE = build_nc()
    res = run_bass_kernel_spmd(_NC_CACHE, in_maps, core_ids=list(range(8)))

    out = np.empty((2, L, H), f32)
    for c in range(8):
        b, q = divmod(c, 4)
        out[b, q * TL:(q + 1) * TL, :] = res.results[c]["out_fm"].T
    return out
